# version 02
# baseline (speedup 1.0000x reference)
import os
import numpy as np
from contextlib import ExitStack
import concourse.bass as bass
import concourse.mybir as mybir
from concourse.bass_utils import run_bass_kernel_spmd

F32 = mybir.dt.float32
BF16 = mybir.dt.bfloat16
F16 = mybir.dt.float16
AL = mybir.AluOpType
AF = mybir.ActivationFunctionType

D = 4096; T = 8192; DL = 512; NJ = 4; N = 16; TC = 512; NCH = T // TC
TSL = 1024; RK = 128; OL = 512; NRANK = 8; NPC = 4; PCT = TSL // NPC
HF = TC // 2

CONSTS = [1.0, -1.0, 0.5, -0.5, 1e-5, 1.0 / 4096.0, -1e30, 0.0]

ENGS = ("sync", "gpsimd", "vector", "scalar", "tensor")


def build():
    nc = bass.Bass("TRN2", target_bir_lowering=False)
    dt_ = nc.dram_tensor

    xT_d = dt_("xT_d", [DL, T], F32, kind="ExternalInput").ap()
    xT_t = dt_("xT_t", [D, TSL], F32, kind="ExternalInput").ap()
    wpdt = dt_("wpdt", [128, 32 * 128], F32, kind="ExternalInput").ap()
    wpbc = dt_("wpbc", [128, 32 * 32], F32, kind="ExternalInput").ap()
    dtw = dt_("dtw", [128, DL], F32, kind="ExternalInput").ap()
    dtb = dt_("dtb", [128, NJ], F32, kind="ExternalInput").ap()
    alog = dt_("alog", [128, NJ * N], F32, kind="ExternalInput").ap()
    dsk = dt_("dsk", [128, NJ], F32, kind="ExternalInput").ap()
    wg = dt_("wg", [128, NJ * 8], F32, kind="ExternalInput").ap()
    wl = dt_("wl", [128, NJ * 128], F32, kind="ExternalInput").ap()
    bw = dt_("bw", [128, 32 * 512], F32, kind="ExternalInput").ap()
    lb = dt_("lb", [128, OL], F32, kind="ExternalInput").ap()
    bb = dt_("bb", [128, NJ], F32, kind="ExternalInput").ap()
    sel16 = dt_("sel16", [16, N * 128], BF16, kind="ExternalInput").ap()
    selW = dt_("selW", [8, 128], F32, kind="ExternalInput").ap()
    selG = dt_("selG", [64, 8], F32, kind="ExternalInput").ap()
    ones16 = dt_("ones16", [128, 1], F16, kind="ExternalInput").ap()
    ones32 = dt_("ones32", [128, 1], F32, kind="ExternalInput").ap()
    cst = dt_("cst", [128, 8], F32, kind="ExternalInput").ap()

    out_t = dt_("out_t", [OL, T], F32, kind="ExternalOutput").ap()
    KDUMP = os.environ.get("KDUMP", "")
    dbg = dt_("dbg", [1280, TSL], F32, kind="ExternalOutput").ap() if KDUMP else None

    ag1_in = dt_("ag1_in", [160, TSL], F32).ap()
    ag1_out = dt_("ag1_out", [160 * NRANK, TSL], F32).ap()
    ag2_in = [dt_(f"ag2i_{c}", [656, TC], BF16).ap() for c in range(NCH)]
    ag2_out = [dt_(f"ag2o_{c}", [656 * NRANK, TC], BF16).ap() for c in range(NCH)]
    ar1_i = [dt_(f"ar1i_{c}", [2, TC], F32).ap() for c in range(NCH)]
    ar1_o = [dt_(f"ar1o_{c}", [2, TC], F32).ap() for c in range(NCH)]
    ar2_i = [dt_(f"ar2i_{c}", [2, TC], F32).ap() for c in range(NCH)]
    ar2_o = [dt_(f"ar2o_{c}", [2, TC], F32).ap() for c in range(NCH)]
    gdram = dt_("gdram", [8, TC], F32).ap()
    wdram = dt_("wdram", [8, TC], F32).ap()

    es = ExitStack()
    sb = lambda name, shape, dty: es.enter_context(nc.sbuf_tensor(name, shape, dty))
    ps = lambda name, shape: es.enter_context(nc.psum_tensor(name, shape, F32))
    sem = lambda name: es.enter_context(nc.semaphore(name))

    SM = {n: sem(n) for n in
          ("sm_ld", "sm_gld", "sm_cc", "sm_pe", "sm_act", "sm_dve", "sm_out",
           "sm_x", "sm_dtin", "sm_bc", "sm_ar1", "sm_ag2", "sm_ar2")}
    CNT = {n: 0 for n in SM}

    sched = {e: [] for e in ENGS}
    WM = {e: {n: 0 for n in SM} for e in ENGS}
    EV = {}

    def emit(eng, fn):
        sched[eng].append(fn)

    def wt(eng, key):
        if key not in EV:
            raise KeyError(f"event {key} not defined (eng {eng})")
        s, v = EV[key]
        if WM[eng][s] < v:
            WM[eng][s] = v
            emit(eng, lambda q, s=s, v=v: q.wait_ge(SM[s], v))

    def ev(key, s, n=1):
        CNT[s] += n
        EV[key] = (s, CNT[s])
        return s, CNT[s]

    def op(eng, build_fn, evkey=None, sname=None, n=1, waits=()):
        for k in waits:
            wt(eng, k)
        if evkey is not None:
            s, v = ev(evkey, sname, n)
            emit(eng, lambda q, b=build_fn, s=s, n=n: b(q).then_inc(SM[s], n))
        else:
            emit(eng, lambda q, b=build_fn: b(q))

    s_basew = sb("s_basew", [128, 32 * 512], BF16)
    s_lorb = sb("s_lorb", [128, OL], BF16)
    s_dtw = sb("s_dtw", [128, DL], BF16)
    s_wg = sb("s_wg", [128, NJ * 8], BF16)
    s_wl = sb("s_wl", [128, NJ * 128], BF16)
    s_sel16 = sb("s_sel16", [16, N * 128], BF16)
    s_selW = sb("s_selW", [8, 128], F32)
    s_selG = sb("s_selG", [64, 8], F32)
    s_ones16 = sb("s_ones16", [128, 1], F16)
    s_ones32 = sb("s_ones32", [128, 1], F32)
    s_cst = sb("s_cst", [128, 8], F32)
    s_A = sb("s_A", [128, NJ * N], F32)
    s_dtb = sb("s_dtb", [128, NJ], F32)
    s_D1 = sb("s_D1", [128, NJ], F32)
    s_bb = sb("s_bb", [128, NJ], F32)

    for i, val in enumerate(CONSTS):
        nc.const_aps.register(F32, val, s_cst[:, i:i + 1])

    s_x = sb("s_x", [128, NJ, TC], F16)
    s_dtin = sb("s_dtin", [128, TC], BF16)
    s_B = sb("s_B", [16, TC], BF16)
    s_C = sb("s_C", [16, TC], BF16)
    s_dt = sb("s_dt", [128, NJ, TC], F16)
    s_u = sb("s_u", [128, TC], F16)
    s_Brep = sb("s_Brep", [128, N, TC], BF16)
    s_Crep = sb("s_Crep", [128, N, TC], BF16)
    s_dA = sb("s_dA", [128, N, TC], F16)
    s_dbx = sb("s_dbx", [128, 4, TC], F16)
    s_h = sb("s_h", [128, N, TC], F16)
    s_state = [sb(f"s_state{k}", [128, NJ * N], F32) for k in range(2)]
    s_y = sb("s_y", [128, TC], F32)
    s_sqa = sb("s_sqa", [128, TC], F32)
    s_sqb = sb("s_sqb", [128, TC], F32)
    s_v = [sb(f"s_v{k}", [128, NJ, TC], F16) for k in range(2)]
    s_t = sb("s_t", [128, TC], F16)
    s_m1 = sb("s_m1", [128, TC], F32)
    s_q1 = sb("s_q1", [128, TC], F32)
    s_r1 = sb("s_r1", [128, TC], F32)
    s_ln1b = [sb(f"s_ln1b{k}", [128, NJ, TC], BF16) for k in range(2)]
    s_gp = sb("s_gp", [8, TC], F32)
    s_lowp = sb("s_lowp", [128, TC], BF16)
    s_rhs = sb("s_rhs", [128, 32, HF], BF16)
    s_lowblk = sb("s_lowblk", [128, 2, TC], BF16)
    s_gbf = sb("s_gbf", [64, TC], F32)
    s_gsum = sb("s_gsum", [8, TC], F32)
    s_gpk = sb("s_gpk", [128, 32], F32)
    s_gsc = sb("s_gsc", [128, 12 * 32], F32)
    s_wpk = sb("s_wpk", [128, 32], F32)
    s_w8 = sb("s_w8", [8, TC], F32)
    s_wrep = sb("s_wrep", [128, TC], BF16)
    s_lowacc = sb("s_lowacc", [128, TC], F32)
    s_lowW = sb("s_lowW", [128, TC], BF16)
    s_ffn = sb("s_ffn", [128, NJ, TC], F16)
    s_w2 = [sb(f"s_w2{k}", [128, NJ, TC], F16) for k in range(2)]
    s_m2 = sb("s_m2", [128, TC], F32)
    s_q2 = sb("s_q2", [128, TC], F32)
    s_r2 = sb("s_r2", [128, TC], F32)
    s_st1 = sb("s_st1", [1, TC], F32)
    s_st2 = sb("s_st2", [1, TC], F32)
    s_out = sb("s_out", [128, TC], F32)

    p_dtr = ps("p_dtr", [128, TC])
    p_rep = ps("p_rep", [128, TC])
    p_stat1 = ps("p_stat1", [1, TC])
    p_gl = ps("p_gl", [128, 2 * TC])
    p_ffn = ps("p_ffn", [128, 2 * TC])
    p_wrep = ps("p_wrep", [128, TC])


    _rhsflat = s_rhs[:, :, :].rearrange("p n t -> p (n t)")
    s_wpdt = _rhsflat[:, 0:32 * 128]
    s_wpbc = _rhsflat[:, 32 * 128:32 * 128 + 32 * 32]
    s_xtt = [s_Brep[:, :, :].rearrange("p n t -> p (n t)"),
             s_Crep[:, :, :].rearrange("p n t -> p (n t)")]
    s_agdt = s_y[:, 0:PCT]
    s_agbc = s_y[0:32, PCT:2 * PCT]
    p0_dt = p_dtr[:, 0:PCT]
    p0_bc = p_rep[0:32, 0:PCT]

    for name, dst, src in [
        ("w_dtb", s_dtb, dtb), ("w_alog", s_A, alog), ("w_dsk", s_D1, dsk),
        ("w_bb", s_bb, bb), ("w_sel16", s_sel16, sel16), ("w_selW", s_selW, selW),
        ("w_selG", s_selG, selG), ("w_o16", s_ones16, ones16),
        ("w_o32", s_ones32, ones32), ("w_cst", s_cst, cst),
    ]:
        op("sync", lambda q, d=dst, s=src: q.dma_start(out=d[:, :], in_=s),
           evkey=name, sname="sm_ld", n=16)
    for name, dst, src in [
        ("w_basew", s_basew, bw), ("w_lorb", s_lorb, lb), ("w_dtw", s_dtw, dtw),
        ("w_wg", s_wg, wg), ("w_wl", s_wl, wl), ("w_wpdt", s_wpdt, wpdt), ("w_wpbc", s_wpbc, wpbc),
    ]:
        op("gpsimd", lambda q, d=dst, s=src: q.dma_start(out=d[:, :], in_=s),
           evkey=name, sname="sm_gld", n=16)

    op("scalar", lambda q: q.activation(s_A[:, :], s_A[:, :], AF.Exp),
       evkey="expA", sname="sm_act", waits=("w_alog", "w_cst"))
    op("vector", lambda q: q.tensor_scalar_mul(s_A[:, :], s_A[:, :], -1.0),
       evkey="negA", sname="sm_dve", waits=("expA",))
    op("vector", lambda q: q.tensor_scalar_add(s_D1[:, :], s_D1[:, :], 1.0),
       evkey="D1", sname="sm_dve", waits=("w_dsk", "w_cst"))

    P0T = int(os.environ.get("P0TRUNC", "9"))
    for pc in range(NPC if P0T >= 2 else 0):
        par = pc % 2
        op("gpsimd", lambda q, pc=pc, par=par: q.dma_start(
            out=s_xtt[par].rearrange("p (j t) -> p j t", j=32),
            in_=xT_t.rearrange("(j p) t -> p j t", p=128)[:, :, pc * PCT:(pc + 1) * PCT]),
           evkey=("xtt", pc), sname="sm_x", n=16,
           waits=(() if pc < 2 else (("p0ev", pc - 2),)))
        NJJ = 32 if P0T != 5 else 1
        if P0T != 4:
            for jj in range(NJJ):
                op("tensor", lambda q, jj=jj, par=par: q.matmul(
                    p0_dt, s_wpdt[:, jj * 128:(jj + 1) * 128],
                    s_xtt[par][:, jj * PCT:(jj + 1) * PCT],
                    start=(jj == 0), stop=(jj == NJJ - 1)),
                   evkey=("p0dt", pc) if jj == NJJ - 1 else None, sname="sm_pe",
                   waits=(("w_wpdt", ("xtt", pc)) if jj == 0 else ()) +
                         ((("p0ev", pc - 1),) if (jj == 0 and pc >= 1) else ()))
            for jj in range(NJJ):
                op("tensor", lambda q, jj=jj, par=par: q.matmul(
                    p0_bc, s_wpbc[:, jj * 32:(jj + 1) * 32],
                    s_xtt[par][:, jj * PCT:(jj + 1) * PCT],
                    start=(jj == 0), stop=(jj == NJJ - 1)),
                   evkey=("p0bc", pc) if jj == NJJ - 1 else None, sname="sm_pe",
                   waits=(("w_wpbc",) if jj == 0 else ()))
            op("vector", lambda q: q.tensor_copy(s_agdt, p0_dt),
               waits=(("p0dt", pc),) + ((("agw", pc - 1, 1),) if pc >= 1 else ()))
            op("vector", lambda q: q.tensor_copy(s_agbc, p0_bc),
               evkey=("p0ev", pc), sname="sm_dve", waits=(("p0bc", pc),))
        else:
            op("vector", lambda q: q.memset(s_agdt, 0.125),
               waits=((("xtt", pc),)) + ((("agw", pc - 1, 1),) if pc >= 1 else ()))
            op("vector", lambda q: q.memset(s_agbc, 0.125),
               evkey=("p0ev", pc), sname="sm_dve")
        op("sync", lambda q, pc=pc: q.dma_start(
            out=ag1_in[0:128, pc * PCT:(pc + 1) * PCT], in_=s_agdt),
           evkey=("agw", pc, 0), sname="sm_ld", n=16, waits=(("p0ev", pc),))
        op("sync", lambda q, pc=pc: q.dma_start(
            out=ag1_in[128:160, pc * PCT:(pc + 1) * PCT], in_=s_agbc),
           evkey=("agw", pc, 1), sname="sm_ld", n=16)
    if P0T >= 2:
        EV["ag1w1"] = EV[("agw", NPC - 1, 0)]
        EV["ag1w2"] = EV[("agw", NPC - 1, 1)]
    else:
        op("sync", lambda q: q.dma_start(
            out=ag1_in[0:128, :], in_=xT_t[0:128, :]),
           evkey="ag1w1", sname="sm_ld", n=16, waits=("w_cst",))
        op("sync", lambda q: q.dma_start(
            out=ag1_in[128:160, :], in_=xT_t[0:32, :]),
           evkey="ag1w2", sname="sm_ld", n=16)
    if P0T >= 3:
        op("gpsimd", lambda q: q.collective_compute(
            "AllGather", AL.bypass, replica_groups=[list(range(NRANK))],
            ins=[ag1_in.opt()], outs=[ag1_out.opt()]),
           evkey="ag1", sname="sm_cc", waits=("ag1w1", "ag1w2"))
    else:
        op("gpsimd", lambda q: q.dma_start(out=ag1_out[0:160, :], in_=ag1_in),
           evkey="ag1", sname="sm_gld", n=16, waits=("ag1w1", "ag1w2"))

    EV["p0_sync"] = EV["ag1w2"]
    EV["p0_gp"] = EV["ag1"]
    EV["p0_dve"] = EV.get(("p0ev", NPC - 1), EV["D1"])
    EV["p0_act"] = EV["expA"]
    EV["p0_pe"] = EV.get(("p0dt", NPC - 1), EV["D1"])


    P0W = ("p0_sync", "p0_gp", "p0_dve", "p0_act", "p0_pe")

    def rot_mask(k):
        return [(l - l % 8) + ((l % 8 + k) % 8) for l in range(32)]

    M4, M2, M1 = rot_mask(4), rot_mask(2), rot_mask(1)

    def stage_A(c):
        rnk, hlf = c // 2, c % 2
        csl = slice(hlf * TC, (hlf + 1) * TC)
        w0 = P0W if c == 0 else ()
        op("gpsimd", lambda q, c=c: q.dma_start(
            out=s_x[:, :, :],
            in_=xT_d.rearrange("(j p) t -> p j t", p=128)[:, :, c * TC:(c + 1) * TC]),
           evkey=("xld", c), sname="sm_x", n=16,
           waits=w0 + ((("vstt", c - 1),) if c >= 1 else ()))
        op("gpsimd", lambda q: q.dma_start(
            out=s_dtin[:, :], in_=ag1_out[160 * rnk:160 * rnk + 128, csl]),
           evkey=("dtinld", c), sname="sm_dtin", n=16,
           waits=w0 + ("ag1",) + ((("dtmm", c - 1, NJ - 1),) if c >= 1 else ()))
        op("gpsimd", lambda q: q.dma_start(
            out=s_B[:, :], in_=ag1_out[160 * rnk + 128:160 * rnk + 144, csl]),
           evkey=("Bld", c), sname="sm_bc", n=16,
           waits=((("repB", c - 1, N - 1),) if c >= 1 else ()))
        op("gpsimd", lambda q: q.dma_start(
            out=s_C[:, :], in_=ag1_out[160 * rnk + 144:160 * rnk + 160, csl]),
           evkey=("Cld", c), sname="sm_bc", n=16,
           waits=((("repC", c - 1, N - 1),) if c >= 1 else ()))
        EV[("Bld", c)] = EV[("Cld", c)]

        for j in range(NJ):
            op("tensor", lambda q, j=j: q.matmul(
                p_dtr[:, :], s_dtw[:, j * 128:(j + 1) * 128], s_dtin[:, :],
                start=True, stop=True),
               evkey=("dtmm", c, j), sname="sm_pe",
               waits=w0 + (("dtinld", c), "w_dtw") +
                     ((("sp", c, j - 1),) if j >= 1 else
                      ((("sp", c - 1, NJ - 1), ("repC", c - 1, N - 1),
                        ("repB", c - 1, N - 2)) if c >= 1 else ())))
            op("scalar", lambda q, j=j: q.activation(
                s_dt[:, j, :], p_dtr[:, :], AF.Exp, bias=s_dtb[:, j:j + 1]),
               waits=(("dtmm", c, j),) + ((("uj", c - 1),) if (c >= 1 and j == 0) else ()))
            op("scalar", lambda q, j=j: q.activation(
                s_dt[:, j, :], s_dt[:, j, :], AF.Ln, bias=1.0))
            op("scalar", lambda q, j=j: q.activation(
                s_gsc[:, 11 * 32 + 1:11 * 32 + 2], s_dt[:, j, 0:1], AF.Copy),
               evkey=("sp", c, j), sname="sm_act")
        for n in range(N):
            bank = p_rep if n % 2 == 0 else p_dtr
            op("tensor", lambda q, n=n, bank=bank: q.matmul(
                bank[:, :], s_sel16[:, n * 128:(n + 1) * 128], s_B[:, :],
                start=True, stop=True),
               evkey=("repBmm", c, n), sname="sm_pe",
               waits=(("Bld", c), "w_sel16") +
                     ((("repC", c - 1, N - 1),) if (n == 0 and c >= 1) else ()) +
                     ((("sp", c, NJ - 1),) if n == 1 else ()) +
                     ((("repB", c, n - 2),) if n >= 2 else ()))
            op("scalar", lambda q, n=n, bank=bank: q.activation(
                s_Brep[:, n, :], bank[:, :], AF.Copy),
               evkey=("repB", c, n), sname="sm_act",
               waits=(("repBmm", c, n),) +
                     ((("dbx", c - 1, NJ - 1),) if (n == 0 and c >= 1) else ()))
        op("scalar", lambda q: q.activation(
            s_gsc[:, 11 * 32 + 1:11 * 32 + 2], s_Brep[:, N - 1, 0:1], AF.Copy),
           evkey=("repBbar", c), sname="sm_act")
        EV[("repB", c, N - 1)] = EV[("repBbar", c)]
        for n in range(N):
            bank = p_rep if n % 2 == 0 else p_dtr
            op("tensor", lambda q, n=n, bank=bank: q.matmul(
                bank[:, :], s_sel16[:, n * 128:(n + 1) * 128], s_C[:, :],
                start=True, stop=True),
               evkey=("repCmm", c, n), sname="sm_pe",
               waits=(("Cld", c), ("repB", c, N - 1)) +
                     ((("repC", c, n - 2),) if n >= 2 else ()))
            op("vector", lambda q, n=n, bank=bank: q.tensor_copy(s_Crep[:, n, :], bank[:, :]),
               evkey=("repC", c, n), sname="sm_dve",
               waits=(("repCmm", c, n),) +
                     ((("hc", c - 1, NJ - 1),) if (n == 0 and c >= 1) else ()))

        for j in range(NJ):
            def emit_dA(g, j=j):
                for n in range(g * 8, g * 8 + 8):
                    op("scalar", lambda q, j=j, n=n: q.activation(
                        s_dA[:, n, :], s_dt[:, j, :], AF.Exp,
                        scale=s_A[:, j * N + n:j * N + n + 1]),
                       sname="sm_act",
                       waits=(("negA",) if (j == 0 and g == 0) else ()) +
                             ((("scang", c, j, g - 1),) if g >= 1 else
                              ((("scang", c, j - 1, 1),) if j >= 1 else
                               ((("scang", c - 1, NJ - 1, 1),) if c >= 1 else ()))))
                op("scalar", lambda q, g=g: q.activation(
                    s_gsc[:, 11 * 32 + 1:11 * 32 + 2],
                    s_dA[:, g * 8 + 7, 0:1], AF.Copy),
                   evkey=("dA", c, j, g), sname="sm_act")

            def emit_quarter(g4, j=j, ub=None):
                op("vector", lambda q, ub=ub, g4=g4: q.tensor_mul(
                    s_dbx[:, :, :], ub, s_Brep[:, g4 * 4:(g4 + 1) * 4, :]),
                   evkey=("dbx", c, j) if g4 == 3 else None, sname="sm_dve",
                   waits=((("repB", c, N - 1),) if g4 == 0 else ()))
                for n in range(g4 * 4, g4 * 4 + 4):
                    op("vector", lambda q, j=j, n=n, g4=g4, c=c: q.tensor_tensor_scan(
                        s_h[:, n, :], s_dA[:, n, :], s_dbx[:, n - g4 * 4, :],
                        s_state[1 - c % 2][:, j * N + n:j * N + n + 1],
                        AL.mult, AL.add),
                       evkey=("scang", c, j, n // 8) if n % 8 == 7 else None,
                       sname="sm_dve",
                       waits=(("dA", c, j, n // 8),) if n % 8 == 0 else ())

            emit_dA(0)
            op("vector", lambda q, j=j: q.tensor_mul(
                s_u[:, :], s_dt[:, j, :], s_x[:, j, :]),
               evkey=("uj", c) if j == NJ - 1 else None, sname="sm_dve",
               waits=(("xld", c), ("sp", c, j)) +
                     ((("hc", c, j - 1),) if j >= 1 else ()))
            ub = s_u[:, :].unsqueeze(1).broadcast_to([128, 4, TC])
            emit_quarter(0, ub=ub)
            emit_quarter(1, ub=ub)
            emit_dA(1)
            emit_quarter(2, ub=ub)
            emit_quarter(3, ub=ub)
            op("vector", lambda q, j=j, c=c: q.tensor_copy(
                s_state[c % 2][:, j * N:(j + 1) * N], s_h[:, :, TC - 1:TC]),
               waits=(("scang", c, j, 1),))
            op("vector", lambda q: q.tensor_mul(s_h[:, :, :], s_h[:, :, :], s_Crep[:, :, :]),
               evkey=("hc", c, j), sname="sm_dve", waits=(("repC", c, N - 1),))
            op("vector", lambda q: q.tensor_add(s_h[:, 0:8, :], s_h[:, 0:8, :], s_h[:, 8:16, :]))
            op("vector", lambda q: q.tensor_add(s_h[:, 0:4, :], s_h[:, 0:4, :], s_h[:, 4:8, :]))
            op("vector", lambda q: q.tensor_add(s_h[:, 0:2, :], s_h[:, 0:2, :], s_h[:, 2:4, :]))
            op("vector", lambda q: q.tensor_add(
                s_y[:, :], s_h[:, 0, :], s_h[:, 1, :]))
            op("vector", lambda q, j=j, c=c: q.scalar_tensor_tensor(
                s_v[c % 2][:, j, :], s_x[:, j, :], s_D1[:, j:j + 1], s_y[:, :],
                AL.mult, AL.add),
               evkey=("vstt", c) if j == NJ - 1 else ("vj", c, j), sname="sm_dve",
               waits=("D1",) +
                     ((("summ", c - 2),) if (j == 0 and c >= 2) else ()))

        for j in range(NJ):
            op("tensor", lambda q, j=j, c=c: q.matmul(
                p_stat1[:, :], s_ones16[:, :], s_v[c % 2][:, j, :],
                start=(j == 0), stop=(j == NJ - 1)),
               evkey=("summ", c) if j == NJ - 1 else None, sname="sm_pe",
               waits=((("vj", c, j),) if j < NJ - 1 else (("vstt", c),)) +
                     ((("sqev", c - 1),) if (j == 0 and c >= 1) else ()) +
                     ((("sqev2", c - 3),) if (j == 0 and c >= 3) else ()) +
                     (("w_o16",) if j == 0 else ()))
        op("vector", lambda q: q.tensor_copy(s_st1[:, :], p_stat1[:, :]),
           evkey=("sumev", c), sname="sm_dve",
           waits=(("summ", c),) + ((("ar2w1", c - 3),) if c >= 3 else ()))
        for j in range(NJ):
            sq = s_sqa if j % 2 == 0 else s_sqb
            op("vector", lambda q, j=j, c=c, sq=sq: q.tensor_mul(
                sq[:, :], s_v[c % 2][:, j, :], s_v[c % 2][:, j, :]),
               evkey=("sq", c, j), sname="sm_dve",
               waits=((("sqmm", c, j - 2),) if j >= 2 else ()) +
                     ((("sqmm2", c - 3, j),) if c >= 3 else ()))
            op("tensor", lambda q, j=j, sq=sq: q.matmul(
                p_stat1[:, :], s_ones32[:, :], sq[:, :],
                start=(j == 0), stop=(j == NJ - 1)),
               evkey=("sqmm", c, j), sname="sm_pe",
               waits=(("sq", c, j),) + ((("sumev", c), "w_o32") if j == 0 else ()))
        op("vector", lambda q: q.tensor_copy(s_st2[:, :], p_stat1[:, :]),
           evkey=("sqev", c), sname="sm_dve",
           waits=(("sqmm", c, NJ - 1),) + ((("ar2w2", c - 3),) if c >= 3 else ()))
        op("sync", lambda q, c=c: q.dma_start(out=ar1_i[c][0:1, :], in_=s_st1[:, :]),
           evkey=("ar1w1", c), sname="sm_ld", n=16, waits=(("sumev", c),) + w0)
        op("sync", lambda q, c=c: q.dma_start(out=ar1_i[c][1:2, :], in_=s_st2[:, :]),
           evkey=("ar1w2", c), sname="sm_ld", n=16, waits=(("sqev", c),))
        op("gpsimd", lambda q, c=c: q.collective_compute(
            "AllReduce", AL.add, replica_groups=[list(range(NRANK))],
            ins=[ar1_i[c].opt()], outs=[ar1_o[c].opt()]),
           evkey=("ar1", c), sname="sm_ar1",
           waits=(("ar1w1", c), ("ar1w2", c)) +
                 ((("ar1", c - 1),) if c >= 1 else ()))

    def dram_row_bcast(apx, row):
        r = apx[row:row + 1, :]
        return r.partition_broadcast(128)

    def stage_B(c):
        op("sync", lambda q, c=c: q.dma_start(
            out=s_m1[:, :], in_=dram_row_bcast(ar1_o[c], 0)),
           evkey=("m1ld", c), sname="sm_ld", n=16,
           waits=(("ar1", c),) +
                 ((("applyB", c - 1), ("rstd1", c - 1)) if c >= 1 else ()))
        op("sync", lambda q, c=c: q.dma_start(
            out=s_q1[:, :], in_=dram_row_bcast(ar1_o[c], 1)),
           evkey=("q1ld", c), sname="sm_ld", n=16)
        op("vector", lambda q: q.tensor_scalar_mul(s_m1[:, :], s_m1[:, :], 1.0 / 4096.0),
           waits=(("m1ld", c), ("q1ld", c)))
        op("vector", lambda q: q.tensor_scalar_mul(s_q1[:, :], s_q1[:, :], 1.0 / 4096.0))
        op("vector", lambda q: q.tensor_mul(s_t[:, :], s_m1[:, :], s_m1[:, :]))
        op("vector", lambda q: q.tensor_sub(s_q1[:, :], s_q1[:, :], s_t[:, :]),
           evkey=("varops", c), sname="sm_dve")
        op("scalar", lambda q: q.activation(s_r1[:, :], s_q1[:, :], AF.Ln, bias=1e-5),
           waits=(("varops", c),) + ((("applyB", c - 1),) if c >= 1 else ()))
        op("scalar", lambda q: q.activation(s_r1[:, :], s_r1[:, :], AF.Exp, scale=-0.5),
           evkey=("rstd1", c), sname="sm_act")
        for j in range(NJ):
            op("vector", lambda q, j=j, c=c: q.tensor_sub(
                s_t[:, :], s_v[c % 2][:, j, :], s_m1[:, :]),
               waits=((("rstd1", c),) if j == 0 else ()) +
                     ((("ag2w1", c - 1), ("lpm", c - 1)) if (j == 0 and c >= 1) else ()))
            op("vector", lambda q, j=j, c=c: q.tensor_mul(
                s_ln1b[c % 2][:, j, :], s_t[:, :], s_r1[:, :]),
               evkey=("applyB", c) if j == NJ - 1 else ("apl", c, j), sname="sm_dve")
        for j in range(NJ):
            op("tensor", lambda q, j=j, c=c: q.matmul(
                p_gl[0:8, 0:TC], s_wg[:, j * 8:(j + 1) * 8], s_ln1b[c % 2][:, j, :],
                start=(j == 0), stop=(j == NJ - 1)),
               evkey=("gpm", c) if j == NJ - 1 else None, sname="sm_pe",
               waits=((("apl", c, j),) if j < NJ - 1 else (("applyB", c),)) +
                     (("w_wg",) if j == 0 else ()) +
                     ((("glev", c - 1),) if (j == 0 and c >= 1) else ()))
        for j in range(NJ):
            op("tensor", lambda q, j=j, c=c: q.matmul(
                p_gl[:, TC:2 * TC], s_wl[:, j * 128:(j + 1) * 128], s_ln1b[c % 2][:, j, :],
                start=(j == 0), stop=(j == NJ - 1)),
               evkey=("lpm", c) if j == NJ - 1 else None, sname="sm_pe",
               waits=(("w_wl",) if j == 0 else ()))
        op("vector", lambda q: q.tensor_copy(s_gp[:, :], p_gl[0:8, 0:TC]),
           evkey=("gpev", c), sname="sm_dve", waits=(("gpm", c),))
        op("vector", lambda q: q.tensor_copy(s_lowp[:, :], p_gl[:, TC:2 * TC]),
           evkey=("glev", c), sname="sm_dve", waits=(("lpm", c),))
        op("sync", lambda q, c=c: q.dma_start(
            out=ag2_in[c][0:512, :].rearrange("(j p) t -> p j t", p=128),
            in_=s_ln1b[c % 2][:, :, :]),
           evkey=("ag2w1", c), sname="sm_ld", n=16, waits=(("applyB", c),))
        op("sync", lambda q, c=c: q.dma_start(
            out=ag2_in[c][512:640, :], in_=s_lowp[:, :]),
           evkey=("ag2w2", c), sname="sm_ld", n=16, waits=(("glev", c),))
        op("sync", lambda q, c=c: q.dma_start(
            out=ag2_in[c][640:656, :].rearrange("(a b) t -> a (b t)", b=2).bitcast(F32),
            in_=s_gp[:, :]),
           evkey=("ag2w3", c), sname="sm_ld", n=16, waits=(("gpev", c),))
        op("gpsimd", lambda q, c=c: q.collective_compute(
            "AllGather", AL.bypass, replica_groups=[list(range(NRANK))],
            ins=[ag2_in[c].opt()], outs=[ag2_out[c].opt()]),
           evkey=("ag2", c), sname="sm_ag2",
           waits=(("ag2w1", c), ("ag2w2", c), ("ag2w3", c)) +
                 ((("ag2", c - 1),) if c >= 1 else ()))

    def stage_C(c):
        for r in range(NRANK):
            op("sync", lambda q, r=r, c=c: q.dma_start(
                out=s_gbf[8 * r:8 * r + 8, :],
                in_=ag2_out[c][656 * r + 640:656 * r + 656, :]
                    .rearrange("(a b) t -> a (b t)", b=2).bitcast(F32)),
               evkey=("gbld", c) if r == NRANK - 1 else ("gbldp", c, r), sname="sm_ld", n=16,
               waits=((("ag2", c),) if r == 0 else ()) +
                     ((("gsmm", c - 1),) if (r == 0 and c >= 1) else ()))
        op("tensor", lambda q: q.matmul(
            p_wrep[0:8, :], s_selG[:, :], s_gbf[:, :], start=True, stop=True),
           evkey=("gsmm", c), sname="sm_pe",
           waits=(("gbld", c), "w_selG") +
                 ((("wrepev", c - 1),) if c >= 1 else ()))
        op("vector", lambda q: q.tensor_copy(s_gsum[:, :], p_wrep[0:8, :]),
           evkey=("gsev", c), sname="sm_dve",
           waits=(("gsmm", c),) + ((("gdw", c - 1),) if c >= 1 else ()))
        op("sync", lambda q: q.dma_start(out=gdram, in_=s_gsum[:, :]),
           evkey=("gdw", c), sname="sm_ld", n=16, waits=(("gsev", c),))
        for e in range(8):
            op("sync", lambda q, e=e: q.dma_start(
                out=s_gpk[e::8, :],
                in_=gdram[e:e + 1, :].rearrange("o (q t) -> (o q) t", t=32)),
               evkey=("gpkld", c) if e == 7 else ("gpkldp", c, e),
               sname="sm_ld", n=16,
               waits=(("gdw", c),) + ((("gateops", c - 1),) if (e == 0 and c >= 1) else ()))
        G = s_gpk
        sc = lambda i: s_gsc[:, i * 32:(i + 1) * 32]
        gseq = [0]

        def gop(fn, waits=(), ev=None, eng="vector"):
            key = ("g", c, gseq[0])
            gseq[0] += 1
            prev = (("g", c, gseq[0] - 2),) if gseq[0] >= 2 else ()
            op(eng, fn, evkey=(ev or key), sname="sm_dve" if eng == "vector" else "sm_act",
               waits=prev + waits)

        gop(lambda q: q.stream_shuffle(sc(0)[:, :], G[:, :], M4), waits=(("gpkld", c),))
        gop(lambda q: q.tensor_max(sc(1)[:, :], G[:, :], sc(0)[:, :]))
        gop(lambda q: q.stream_shuffle(sc(0)[:, :], sc(1)[:, :], M2))
        gop(lambda q: q.tensor_max(sc(1)[:, :], sc(1)[:, :], sc(0)[:, :]))
        gop(lambda q: q.stream_shuffle(sc(0)[:, :], sc(1)[:, :], M1))
        gop(lambda q: q.tensor_max(sc(1)[:, :], sc(1)[:, :], sc(0)[:, :]))
        gop(lambda q: q.tensor_tensor(sc(2)[:, :], G[:, :], sc(1)[:, :], AL.is_equal))
        gop(lambda q: q.scalar_tensor_tensor(
            sc(3)[:, :], sc(2)[:, :], -1e30, G[:, :], AL.mult, AL.add))
        gop(lambda q: q.stream_shuffle(sc(0)[:, :], sc(3)[:, :], M4))
        gop(lambda q: q.tensor_max(sc(3)[:, :], sc(3)[:, :], sc(0)[:, :]))
        gop(lambda q: q.stream_shuffle(sc(0)[:, :], sc(3)[:, :], M2))
        gop(lambda q: q.tensor_max(sc(3)[:, :], sc(3)[:, :], sc(0)[:, :]))
        gop(lambda q: q.stream_shuffle(sc(0)[:, :], sc(3)[:, :], M1))
        gop(lambda q: q.tensor_max(sc(3)[:, :], sc(3)[:, :], sc(0)[:, :]))
        gop(lambda q: q.tensor_sub(sc(4)[:, :], sc(3)[:, :], sc(1)[:, :]))
        gop(lambda q: q.activation(sc(5)[:, :], sc(4)[:, :], AF.Exp), eng="scalar")
        gop(lambda q: q.tensor_scalar(
            sc(5)[:, :], sc(5)[:, :], 0.5, 0.5, AL.mult, AL.add))
        gop(lambda q: q.reciprocal(sc(6)[:, :], sc(5)[:, :]))
        gop(lambda q: q.tensor_sub(sc(7)[:, :], G[:, :], sc(1)[:, :]))
        gop(lambda q: q.activation(sc(8)[:, :], sc(7)[:, :], AF.Exp), eng="scalar")
        gop(lambda q: q.tensor_tensor(sc(9)[:, :], G[:, :], sc(3)[:, :], AL.is_ge))
        gop(lambda q: q.tensor_mul(sc(10)[:, :], sc(8)[:, :], sc(9)[:, :]))
        gop(lambda q: q.tensor_mul(s_wpk[:, :], sc(10)[:, :], sc(6)[:, :]),
            waits=((("wdw", c - 1),) if c >= 1 else ()), ev=("gateops", c))
        for e in range(8):
            op("sync", lambda q, e=e: q.dma_start(
                out=wdram[e:e + 1, :].rearrange("o (q t) -> (o q) t", t=32),
                in_=s_wpk[e::8, :]),
               evkey=("wdw", c) if e == 7 else ("wdwp", c, e),
               sname="sm_ld", n=16, waits=(("gateops", c),) if e == 0 else ())
        op("sync", lambda q: q.dma_start(out=s_w8[:, :], in_=wdram),
           evkey=("w8ld", c), sname="sm_ld", n=16,
           waits=(("wdw", c),) + ((("wrepmm", c - 1),) if c >= 1 else ()))
        op("tensor", lambda q: q.matmul(
            p_wrep[:, :], s_selW[:, :], s_w8[:, :], start=True, stop=True),
           evkey=("wrepmm", c), sname="sm_pe",
           waits=(("w8ld", c), "w_selW", ("gsev", c)))
        op("vector", lambda q: q.tensor_copy(s_wrep[:, :], p_wrep[:, :]),
           evkey=("wrepev", c), sname="sm_dve", waits=(("wrepmm", c),))
        for w in range(4):
            for k in range(2):
                r = 2 * w + k
                op("sync", lambda q, r=r, k=k, c=c: q.dma_start(
                    out=s_lowblk[:, k, :],
                    in_=ag2_out[c][656 * r + 512:656 * r + 640, :]),
                   evkey=("lwld", c, w) if k == 1 else ("lwldp", c, w, k), sname="sm_ld", n=16,
                   waits=((("ag2", c),) if (w == 0 and k == 0) else ()) +
                         ((("lws", c, w - 1),) if (w >= 1 and k == 0) else ()) +
                         ((("lws", c - 1, 3),) if (w == 0 and k == 0 and c >= 1) else ()))
            if w == 0:
                op("vector", lambda q: q.tensor_add(
                    s_lowacc[:, :], s_lowblk[:, 0, :], s_lowblk[:, 1, :]),
                   evkey=("lws", c, 0), sname="sm_dve", waits=(("lwld", c, 0),))
            else:
                op("vector", lambda q, w=w, c=c: q.tensor_add(
                    s_lowacc[:, :], s_lowacc[:, :], s_lowblk[:, 0, :]),
                   waits=(("lwld", c, w),))
                op("vector", lambda q, w=w, c=c: q.tensor_add(
                    s_lowacc[:, :], s_lowacc[:, :], s_lowblk[:, 1, :]),
                   evkey=("lws", c, w), sname="sm_dve")
        op("vector", lambda q: q.tensor_mul(s_lowW[:, :], s_lowacc[:, :], s_wrep[:, :]),
           evkey=("lowW", c), sname="sm_dve",
           waits=(("lws", c, 3), ("wrepev", c)) +
                 ((("ffnmm", c - 1, 1, NJ - 1),) if c >= 1 else ()))

        for h in range(2):
            tsl2 = slice(h * HF, (h + 1) * HF)
            for r in range(NRANK):
                op("sync", lambda q, r=r, c=c, h=h: q.dma_start(
                    out=s_rhs[:, 4 * r:4 * r + 4, :],
                    in_=ag2_out[c][656 * r:656 * r + 512, h * HF:(h + 1) * HF]
                        .rearrange("(j p) t -> p j t", p=128)),
                   evkey=("rhsld", c, h) if r == NRANK - 1 else ("rhsldp", c, h, r),
                   sname="sm_ld", n=16,
                   waits=((("ag2", c),) if r == 0 else ()) +
                         ((("ffnmm", c, h - 1, 3),) if (r == 0 and h == 1) else ()) +
                         ((("ffnmm", c - 1, 1, 3),) if (r == 0 and h == 0 and c >= 1) else ()))
            for i in range(NJ):
                reg = slice((i % 2) * TC, (i % 2) * TC + HF)
                for jj in range(32):
                    op("tensor", lambda q, i=i, jj=jj, reg=reg: q.matmul(
                        p_ffn[:, reg], s_basew[:, jj * 512 + i * 128:jj * 512 + (i + 1) * 128],
                        s_rhs[:, jj, :], start=(jj == 0), stop=False),
                       sname="sm_pe",
                       waits=((("rhsld", c, h), "w_basew") if (i == 0 and jj == 0) else ()) +
                             ((("ffnev", c, h, i - 2),) if (jj == 0 and i >= 2) else ()) +
                             ((("ffnev", c, 0, i + 2),) if (jj == 0 and i < 2 and h == 1) else ()) +
                             ((("ffnev", c - 1, 1, i + 2),) if (jj == 0 and i < 2 and h == 0 and c >= 1) else ()))
                op("tensor", lambda q, i=i, reg=reg, tsl2=tsl2: q.matmul(
                    p_ffn[:, reg], s_lorb[:, i * 128:(i + 1) * 128],
                    s_lowW[:, tsl2], start=False, stop=True),
                   evkey=("ffnmm", c, h, i), sname="sm_pe",
                   waits=(("lowW", c), "w_lorb"))
                op("vector", lambda q, i=i, reg=reg, tsl2=tsl2: q.tensor_scalar(
                    s_ffn[:, i, tsl2], p_ffn[:, reg], s_bb[:, i:i + 1], None, AL.add),
                   evkey=("ffnev", c, h, i), sname="sm_dve",
                   waits=(("ffnmm", c, h, i), "w_bb"))
            op("vector", lambda q, c=c, tsl2=tsl2: q.tensor_add(
                s_w2[c % 2][:, :, tsl2], s_ffn[:, :, tsl2], s_ln1b[c % 2][:, :, tsl2]),
               evkey=("resid", c, h), sname="sm_dve",
               waits=(("ffnev", c, h, NJ - 1),) +
                     ((("sqmm2", c - 2, NJ - 1),) if (h == 0 and c >= 2) else ()))
        for j in range(NJ):
            op("tensor", lambda q, j=j, c=c: q.matmul(
                p_stat1[:, :], s_ones16[:, :], s_w2[c % 2][:, j, :],
                start=(j == 0), stop=(j == NJ - 1)),
               evkey=("summ2", c) if j == NJ - 1 else None, sname="sm_pe",
               waits=((("resid", c, 1),) if j == 0 else ()) +
                     ((("sqev", c + 2),) if (j == 0 and c + 2 < NCH) else ()) +
                     ((("sqev2", c - 1),) if (j == 0 and c >= 1) else ()))
        op("vector", lambda q: q.tensor_copy(s_st1[:, :], p_stat1[:, :]),
           evkey=("sumev2", c), sname="sm_dve",
           waits=(("summ2", c),) + ((("ar1w1", c + 2),) if c + 2 < NCH else ()))
        for j in range(NJ):
            sq = s_sqa if j % 2 == 0 else s_sqb
            op("vector", lambda q, j=j, c=c, sq=sq: q.tensor_mul(
                sq[:, :], s_w2[c % 2][:, j, :], s_w2[c % 2][:, j, :]),
               evkey=("sq2", c, j), sname="sm_dve",
               waits=((("sqmm2", c, j - 2),) if j >= 2 else ()) +
                     ((("sqmm", c + 2, j),) if (j == 0 and c + 2 < NCH) else ()))
            op("tensor", lambda q, j=j, sq=sq: q.matmul(
                p_stat1[:, :], s_ones32[:, :], sq[:, :],
                start=(j == 0), stop=(j == NJ - 1)),
               evkey=("sqmm2", c, j), sname="sm_pe",
               waits=(("sq2", c, j),) + ((("sumev2", c),) if j == 0 else ()))
        op("vector", lambda q: q.tensor_copy(s_st2[:, :], p_stat1[:, :]),
           evkey=("sqev2", c), sname="sm_dve",
           waits=(("sqmm2", c, NJ - 1),) + ((("ar1w2", c + 2),) if c + 2 < NCH else ()))
        op("sync", lambda q, c=c: q.dma_start(out=ar2_i[c][0:1, :], in_=s_st1[:, :]),
           evkey=("ar2w1", c), sname="sm_ld", n=16, waits=(("sumev2", c),))
        op("sync", lambda q, c=c: q.dma_start(out=ar2_i[c][1:2, :], in_=s_st2[:, :]),
           evkey=("ar2w2", c), sname="sm_ld", n=16, waits=(("sqev2", c),))
        op("gpsimd", lambda q, c=c: q.collective_compute(
            "AllReduce", AL.add, replica_groups=[list(range(NRANK))],
            ins=[ar2_i[c].opt()], outs=[ar2_o[c].opt()]),
           evkey=("ar2", c), sname="sm_ar2",
           waits=(("ar2w1", c), ("ar2w2", c)) +
                 ((("ar2", c - 1),) if c >= 1 else ()))

    def stage_D(c):
        op("sync", lambda q, c=c: q.dma_start(
            out=s_m2[:, :], in_=dram_row_bcast(ar2_o[c], 0)),
           evkey=("m2ld", c), sname="sm_ld", n=16,
           waits=(("ar2", c),) +
                 ((("applyD", c - 1, NJ - 1), ("rstd2", c - 1)) if c >= 1 else ()))
        op("sync", lambda q, c=c: q.dma_start(
            out=s_q2[:, :], in_=dram_row_bcast(ar2_o[c], 1)),
           evkey=("q2ld", c), sname="sm_ld", n=16)
        op("vector", lambda q: q.tensor_scalar_mul(s_m2[:, :], s_m2[:, :], 1.0 / 4096.0),
           waits=(("m2ld", c), ("q2ld", c)))
        op("vector", lambda q: q.tensor_scalar_mul(s_q2[:, :], s_q2[:, :], 1.0 / 4096.0))
        op("vector", lambda q: q.tensor_mul(s_t[:, :], s_m2[:, :], s_m2[:, :]))
        op("vector", lambda q: q.tensor_sub(s_q2[:, :], s_q2[:, :], s_t[:, :]),
           evkey=("varops2", c), sname="sm_dve")
        op("scalar", lambda q: q.activation(s_r2[:, :], s_q2[:, :], AF.Ln, bias=1e-5),
           waits=(("varops2", c),) + ((("rstd1", c + 2),) if c + 2 < NCH else ()))
        op("scalar", lambda q: q.activation(s_r2[:, :], s_r2[:, :], AF.Exp, scale=-0.5),
           evkey=("rstd2", c), sname="sm_act")
        for j in range(NJ):
            op("vector", lambda q, j=j, c=c: q.tensor_sub(
                s_t[:, :], s_w2[c % 2][:, j, :], s_m2[:, :]),
               waits=((("rstd2", c),) if j == 0 else ()) +
                     ((("applyB", c + 2),) if (j == 0 and c + 2 < NCH) else ()) +
                     ((("outw", c - 1, j),) if c >= 1 else ()))
            op("vector", lambda q, j=j: q.tensor_mul(s_out[:, :], s_t[:, :], s_r2[:, :]),
               evkey=("applyD", c, j), sname="sm_dve",
               waits=(((("outw", c, j - 1),) if j >= 1 else
                       ((("outw", c - 1, NJ - 1),) if c >= 1 else ()))))
            op("sync", lambda q, c=c, j=j: q.dma_start(
                out=out_t.rearrange("(j p) t -> p j t", p=128)[:, j, c * TC:(c + 1) * TC],
                in_=s_out[:, :]),
               evkey=("outw", c, j), sname="sm_out", n=16, waits=(("applyD", c, j),))
        EV[("applyD", c)] = EV[("applyD", c, NJ - 1)]

    op("vector", lambda q: q.memset(s_state[0][:, :], 0.0), waits=P0W)
    op("vector", lambda q: q.memset(s_state[1][:, :], 0.0),
       evkey="st0", sname="sm_dve")
    EV[("scang", -1, NJ - 1, 1)] = EV["st0"]

    LIM = int(os.environ.get("KTRUNC", str(NCH + 3)))
    for i in range(min(NCH + 3, LIM)):
        if i < NCH:
            stage_A(i)
        if 0 <= i - 1 < NCH:
            stage_B(i - 1)
        if 0 <= i - 2 < NCH:
            stage_C(i - 2)
        if 0 <= i - 3 < NCH:
            stage_D(i - 3)

    if KDUMP == "ag1":
        op("gpsimd", lambda q: q.dma_start(out=dbg, in_=ag1_out),
           evkey="dump", sname="sm_gld", n=16, waits=("ag1",))
    elif KDUMP in ("dt", "v", "ln1b", "h", "u", "brep", "w8", "ffn", "w2", "dA", "gsum", "gpk", "wpk", "gbf", "gsc", "r2", "m2", "outd", "state0"):
        srcmap = {
            "dt": (s_dt[:, :, :].rearrange("p j t -> p (j t)"), 128, 2048, "vstt"),
            "v": (s_v[0][:, :, :].rearrange("p j t -> p (j t)"), 128, 2048, "vstt"),
            "h": (s_h[:, :, :].rearrange("p n t -> p (n t)"), 128, 8192, "vstt"),
            "dA": (s_dA[:, :, :].rearrange("p n t -> p (n t)"), 128, 8192, "vstt"),
            "u": (s_u[:, :], 128, 512, "vstt"),
            "brep": (s_Brep[:, :, :].rearrange("p n t -> p (n t)"), 128, 8192, "vstt"),
            "ln1b": (s_ln1b[0][:, :, :].rearrange("p j t -> p (j t)"), 128, 2048, ("applyB", 0)),
            "w8": (s_w8[:, :], 8, 512, ("w8ld", 0)),
            "gsum": (s_gsum[:, :], 8, 512, ("gsev", 0)),
            "gpk": (s_gpk[:, :], 128, 32, ("gpkld", 0)),
            "wpk": (s_wpk[:, :], 128, 32, ("gateops", 0)),
            "gbf": (s_gbf[:, :], 64, 512, ("gbld", 0)),
            "gsc": (s_gsc[:, :], 128, 384, ("gateops", 0)),
            "r2": (s_r2[:, :], 128, 512, ("rstd2", 0)),
            "m2": (s_m2[:, :], 128, 512, ("rstd2", 0)),
            "outd": (s_out[:, :], 128, 512, ("applyD", 0, 3)),
            "state0": (s_state[0][:, :], 128, 64, ("vstt", 0)),
            "ffn": (s_ffn[:, :, :].rearrange("p j t -> p (j t)"), 128, 2048, ("resid", 0, 1)),
            "w2": (s_w2[0][:, :, :].rearrange("p j t -> p (j t)"), 128, 2048, ("resid", 0, 1)),
        }
        sap, pp, ff, wk = srcmap[KDUMP]
        if wk == "vstt":
            wk = ("vstt", 0)
        nrow = (pp * ff * 4 // 4) // TSL
        op("gpsimd", lambda q, sap=sap, pp=pp, ff=ff: q.dma_start(
            out=dbg[0:pp, 0:ff] if ff <= TSL else
                dbg[0:pp * (ff // TSL), :].rearrange("(p k) t -> p (k t)", k=ff // TSL),
            in_=sap), evkey="dump", sname="sm_gld", n=16, waits=(wk,))

    def _final(q):
        q.wait_ge(SM["sm_out"], CNT["sm_out"])
    sched["sync"].append(_final)

    with nc.Block() as blk:
        @blk.sync
        def _(q):
            for fn in sched["sync"]:
                fn(q)

        @blk.gpsimd
        def _(q):
            for fn in sched["gpsimd"]:
                fn(q)

        @blk.vector
        def _(q):
            for fn in sched["vector"]:
                fn(q)

        @blk.scalar
        def _(q):
            for fn in sched["scalar"]:
                fn(q)

        @blk.tensor
        def _(q):
            for fn in sched["tensor"]:
                fn(q)

    es.close()
    return nc


_NC_CACHE = None


def kernel(L0_attn_x, L0_attn_A_log, L0_attn_D, L0_attn_x_proj_w,
           L0_attn_dt_proj_w, L0_attn_dt_proj_b, L0_ffn_gate_w,
           L0_ffn_base_w, L0_ffn_base_b, L0_ffn_lora_A, L0_ffn_lora_B):
    f32 = np.float32
    x = np.asarray(L0_attn_x, f32)[0]
    xT = np.ascontiguousarray(x.T)
    xpw = np.asarray(L0_attn_x_proj_w, f32)
    dtpw = np.asarray(L0_attn_dt_proj_w, f32)
    dtpb = np.asarray(L0_attn_dt_proj_b, f32)
    alogf = np.asarray(L0_attn_A_log, f32)
    dskf = np.asarray(L0_attn_D, f32)
    gwf = np.asarray(L0_ffn_gate_w, f32)
    bwf = np.asarray(L0_ffn_base_w, f32)
    bbf = np.asarray(L0_ffn_base_b, f32)
    lAf = np.asarray(L0_ffn_lora_A, f32)
    lBf = np.asarray(L0_ffn_lora_B, f32)

    wpdt = np.ascontiguousarray(
        xpw[0:128].reshape(128, 32, 128).transpose(2, 1, 0).reshape(128, 32 * 128))
    wpbc = np.ascontiguousarray(
        xpw[128:160].reshape(32, 32, 128).transpose(2, 1, 0).reshape(128, 32 * 32))
    import ml_dtypes
    sel16 = np.zeros((16, N * 128), ml_dtypes.bfloat16)
    for n in range(N):
        sel16[n, n * 128:(n + 1) * 128] = 1.0
    selWh = np.zeros((8, 128), f32)
    for p in range(128):
        selWh[p // 16, p] = 1.0
    selGh = np.zeros((64, 8), f32)
    for r in range(8):
        for e in range(8):
            selGh[8 * r + e, e] = 1.0
    ones16 = np.ones((128, 1), np.float16)
    ones32 = np.ones((128, 1), f32)
    csth = np.tile(np.asarray([CONSTS], f32), (128, 1))
    lA2 = lAf.reshape(128, D)

    in_maps = []
    for k in range(NRANK):
        dsl = slice(DL * k, DL * (k + 1))
        osl = dsl
        tsl = slice(TSL * k, TSL * (k + 1))
        bw_k = np.ascontiguousarray(
            bwf[osl, :].T.reshape(32, 128, 4, 128).transpose(1, 0, 2, 3)
            .reshape(128, 32 * 512))
        lb_k = np.ascontiguousarray(
            lBf[:, osl, :].transpose(0, 2, 1).reshape(128, OL))
        m = dict(
            xT_d=np.ascontiguousarray(xT[dsl, :]),
            xT_t=np.ascontiguousarray(xT[:, tsl]),
            wpdt=wpdt, wpbc=wpbc,
            dtw=np.ascontiguousarray(dtpw[dsl, :].T),
            dtb=np.ascontiguousarray(dtpb[dsl].reshape(NJ, 128).T),
            alog=np.ascontiguousarray(
                alogf[dsl].reshape(NJ, 128, N).transpose(1, 0, 2).reshape(128, NJ * N)),
            dsk=np.ascontiguousarray(dskf[dsl].reshape(NJ, 128).T),
            wg=np.ascontiguousarray(
                gwf[:, dsl].T.reshape(NJ, 128, 8).transpose(1, 0, 2).reshape(128, NJ * 8)),
            wl=np.ascontiguousarray(
                lA2[:, dsl].T.reshape(NJ, 128, 128).transpose(1, 0, 2).reshape(128, NJ * 128)),
            bw=bw_k, lb=lb_k,
            bb=np.ascontiguousarray(bbf[osl].reshape(NJ, 128).T),
            sel16=sel16, selW=selWh, selG=selGh,
            ones16=ones16, ones32=ones32, cst=csth,
        )
        in_maps.append(m)

    nc = build()
    res = run_bass_kernel_spmd(nc, in_maps, core_ids=list(range(NRANK)))
    kernel._res = res
    parts = [np.asarray(res.results[k]["out_t"], f32) for k in range(NRANK)]
    full = np.concatenate(parts, axis=0)
    return np.ascontiguousarray(full.T)[None]


# version 03
# speedup vs baseline: 1.0589x; 1.0589x over previous
import os
import numpy as np
from contextlib import ExitStack
import concourse.bass as bass
import concourse.mybir as mybir
from concourse.bass_utils import run_bass_kernel_spmd

F32 = mybir.dt.float32
BF16 = mybir.dt.bfloat16
F16 = mybir.dt.float16
AL = mybir.AluOpType
AF = mybir.ActivationFunctionType

D = 4096; T = 8192; DL = 512; NJ = 4; N = 16; TC = 512; NCH = T // TC
TSL = 1024; RK = 128; OL = 512; NRANK = 8; NPC = 4; PCT = TSL // NPC
HF = TC // 2

CONSTS = [1.0, -1.0, 0.5, -0.5, 1e-5, 1.0 / 4096.0, -1e30, 0.0]

ENGS = ("sync", "gpsimd", "vector", "scalar", "tensor")


def build():
    nc = bass.Bass("TRN2", target_bir_lowering=False)
    dt_ = nc.dram_tensor

    xT_d = dt_("xT_d", [DL, T], F32, kind="ExternalInput").ap()
    xT_t = dt_("xT_t", [D, TSL], F32, kind="ExternalInput").ap()
    wpdt = dt_("wpdt", [128, 32 * 128], F32, kind="ExternalInput").ap()
    wpbc = dt_("wpbc", [128, 32 * 32], F32, kind="ExternalInput").ap()
    dtw = dt_("dtw", [128, DL], F32, kind="ExternalInput").ap()
    dtb = dt_("dtb", [128, NJ], F32, kind="ExternalInput").ap()
    alog = dt_("alog", [128, NJ * N], F32, kind="ExternalInput").ap()
    dsk = dt_("dsk", [128, NJ], F32, kind="ExternalInput").ap()
    wg = dt_("wg", [128, NJ * 8], F32, kind="ExternalInput").ap()
    wl = dt_("wl", [128, NJ * 128], F32, kind="ExternalInput").ap()
    bw = dt_("bw", [128, 32 * 512], F32, kind="ExternalInput").ap()
    lb = dt_("lb", [128, OL], F32, kind="ExternalInput").ap()
    bb = dt_("bb", [128, NJ], F32, kind="ExternalInput").ap()
    sel16 = dt_("sel16", [16, N * 128], BF16, kind="ExternalInput").ap()
    selW = dt_("selW", [8, 128], F32, kind="ExternalInput").ap()
    selG = dt_("selG", [64, 8], F32, kind="ExternalInput").ap()
    ones16 = dt_("ones16", [128, 1], F16, kind="ExternalInput").ap()
    ones32 = dt_("ones32", [128, 1], F32, kind="ExternalInput").ap()
    cst = dt_("cst", [128, 8], F32, kind="ExternalInput").ap()

    out_t = dt_("out_t", [OL, T], F32, kind="ExternalOutput").ap()
    KDUMP = os.environ.get("KDUMP", "")
    dbg = dt_("dbg", [1280, TSL], F32, kind="ExternalOutput").ap() if KDUMP else None

    ag1_in = dt_("ag1_in", [160, TSL], F32).ap()
    ag1_out = dt_("ag1_out", [160 * NRANK, TSL], F32).ap()
    ag2_in = [dt_(f"ag2i_{c}", [656, TC], BF16).ap() for c in range(NCH)]
    ag2_out = [dt_(f"ag2o_{c}", [656 * NRANK, TC], BF16).ap() for c in range(NCH)]
    ar1_i = [dt_(f"ar1i_{c}", [2, TC], F32).ap() for c in range(NCH)]
    ar1_o = [dt_(f"ar1o_{c}", [2, TC], F32).ap() for c in range(NCH)]
    ar2_i = [dt_(f"ar2i_{c}", [2, TC], F32).ap() for c in range(NCH)]
    ar2_o = [dt_(f"ar2o_{c}", [2, TC], F32).ap() for c in range(NCH)]
    gdram = dt_("gdram", [8, TC], F32).ap()
    wdram = dt_("wdram", [8, TC], F32).ap()

    es = ExitStack()
    sb = lambda name, shape, dty: es.enter_context(nc.sbuf_tensor(name, shape, dty))
    ps = lambda name, shape: es.enter_context(nc.psum_tensor(name, shape, F32))
    sem = lambda name: es.enter_context(nc.semaphore(name))

    SM = {n: sem(n) for n in
          ("sm_ld", "sm_gld", "sm_cc", "sm_pe", "sm_act", "sm_dve", "sm_out",
           "sm_x", "sm_dtin", "sm_bc", "sm_ar1", "sm_ag2", "sm_ar2")}
    CNT = {n: 0 for n in SM}

    sched = {e: [] for e in ENGS}
    WM = {e: {n: 0 for n in SM} for e in ENGS}
    EV = {}

    def emit(eng, fn):
        sched[eng].append(fn)

    def wt(eng, key):
        if key not in EV:
            raise KeyError(f"event {key} not defined (eng {eng})")
        s, v = EV[key]
        if WM[eng][s] < v:
            WM[eng][s] = v
            emit(eng, lambda q, s=s, v=v: q.wait_ge(SM[s], v))

    def ev(key, s, n=1):
        CNT[s] += n
        EV[key] = (s, CNT[s])
        return s, CNT[s]

    def op(eng, build_fn, evkey=None, sname=None, n=1, waits=()):
        for k in waits:
            wt(eng, k)
        if evkey is not None:
            s, v = ev(evkey, sname, n)
            emit(eng, lambda q, b=build_fn, s=s, n=n: b(q).then_inc(SM[s], n))
        else:
            emit(eng, lambda q, b=build_fn: b(q))

    s_basew = sb("s_basew", [128, 32 * 512], BF16)
    s_lorb = sb("s_lorb", [128, OL], BF16)
    s_dtw = sb("s_dtw", [128, DL], BF16)
    s_wg = sb("s_wg", [128, NJ * 8], BF16)
    s_wl = sb("s_wl", [128, NJ * 128], BF16)
    s_sel16 = sb("s_sel16", [16, N * 128], BF16)
    s_selW = sb("s_selW", [8, 128], F32)
    s_selG = sb("s_selG", [64, 8], F32)
    s_ones16 = sb("s_ones16", [128, 1], F16)
    s_ones32 = sb("s_ones32", [128, 1], F32)
    s_cst = sb("s_cst", [128, 8], F32)
    s_A = sb("s_A", [128, NJ * N], F32)
    s_dtb = sb("s_dtb", [128, NJ], F32)
    s_D1 = sb("s_D1", [128, NJ], F32)
    s_bb = sb("s_bb", [128, NJ], F32)

    for i, val in enumerate(CONSTS):
        nc.const_aps.register(F32, val, s_cst[:, i:i + 1])

    s_x = sb("s_x", [128, NJ, TC], F16)
    s_dtin = sb("s_dtin", [128, TC], BF16)
    s_B = sb("s_B", [16, TC], BF16)
    s_C = sb("s_C", [16, TC], BF16)
    s_dt = sb("s_dt", [128, NJ, TC], F16)
    s_u = sb("s_u", [128, TC], F16)
    s_Brep = sb("s_Brep", [128, N, TC], BF16)
    s_Crep = sb("s_Crep", [128, N, TC], BF16)
    s_dA = sb("s_dA", [128, N, TC], F16)
    s_dbx = sb("s_dbx", [128, 4, TC], F16)
    s_h = sb("s_h", [128, N, TC], F16)
    s_state = [sb(f"s_state{k}", [128, NJ * N], F32) for k in range(2)]
    s_y = sb("s_y", [128, TC], F32)
    s_sqa = sb("s_sqa", [128, TC], F32)
    s_sqb = sb("s_sqb", [128, TC], F32)
    s_v = [sb(f"s_v{k}", [128, NJ, TC], F16) for k in range(2)]
    s_t = sb("s_t", [128, TC], F16)
    s_m1 = sb("s_m1", [128, TC], F32)
    s_q1 = sb("s_q1", [128, TC], F32)
    s_r1 = sb("s_r1", [128, TC], F32)
    s_ln1b = [sb(f"s_ln1b{k}", [128, NJ, TC], BF16) for k in range(2)]
    s_gp = sb("s_gp", [8, TC], F32)
    s_lowp = sb("s_lowp", [128, TC], BF16)
    s_rhs = sb("s_rhs", [128, 32, HF], BF16)
    s_lowblk = sb("s_lowblk", [128, 2, TC], BF16)
    s_gbf = sb("s_gbf", [64, TC], F32)
    s_gsum = sb("s_gsum", [8, TC], F32)
    s_gpk = sb("s_gpk", [128, 32], F32)
    s_gsc = sb("s_gsc", [128, 12 * 32], F32)
    s_wpk = sb("s_wpk", [128, 32], F32)
    s_w8 = sb("s_w8", [8, TC], F32)
    s_wrep = sb("s_wrep", [128, TC], BF16)
    s_lowacc = sb("s_lowacc", [128, TC], F32)
    s_lowW = sb("s_lowW", [128, TC], BF16)
    s_ffn = sb("s_ffn", [128, NJ, TC], F16)
    s_w2 = [sb(f"s_w2{k}", [128, NJ, TC], F16) for k in range(2)]
    s_m2 = sb("s_m2", [128, TC], F32)
    s_q2 = sb("s_q2", [128, TC], F32)
    s_r2 = sb("s_r2", [128, TC], F32)
    s_st1 = sb("s_st1", [1, TC], F32)
    s_st2 = sb("s_st2", [1, TC], F32)
    s_out = sb("s_out", [128, TC], F32)

    p_dtr = ps("p_dtr", [128, TC])
    p_rep = ps("p_rep", [128, TC])
    p_stat1 = ps("p_stat1", [1, TC])
    p_gl = ps("p_gl", [128, 2 * TC])
    p_ffn = ps("p_ffn", [128, 2 * TC])
    p_wrep = ps("p_wrep", [128, TC])


    _rhsflat = s_rhs[:, :, :].rearrange("p n t -> p (n t)")
    s_wpdt = _rhsflat[:, 0:32 * 128]
    s_wpbc = _rhsflat[:, 32 * 128:32 * 128 + 32 * 32]
    s_xtt = [s_Brep[:, :, :].rearrange("p n t -> p (n t)"),
             s_Crep[:, :, :].rearrange("p n t -> p (n t)")]
    s_agdt = s_y[:, 0:PCT]
    s_agbc = s_y[0:32, PCT:2 * PCT]
    p0_dt = p_dtr[:, 0:PCT]
    p0_bc = p_rep[0:32, 0:PCT]

    for name, dst, src in [
        ("w_dtb", s_dtb, dtb), ("w_alog", s_A, alog), ("w_dsk", s_D1, dsk),
        ("w_bb", s_bb, bb), ("w_sel16", s_sel16, sel16), ("w_selW", s_selW, selW),
        ("w_selG", s_selG, selG), ("w_o16", s_ones16, ones16),
        ("w_o32", s_ones32, ones32), ("w_cst", s_cst, cst),
    ]:
        op("sync", lambda q, d=dst, s=src: q.dma_start(out=d[:, :], in_=s),
           evkey=name, sname="sm_ld", n=16)
    for name, dst, src in [
        ("w_basew", s_basew, bw), ("w_lorb", s_lorb, lb), ("w_dtw", s_dtw, dtw),
        ("w_wg", s_wg, wg), ("w_wl", s_wl, wl), ("w_wpdt", s_wpdt, wpdt), ("w_wpbc", s_wpbc, wpbc),
    ]:
        op("gpsimd", lambda q, d=dst, s=src: q.dma_start(out=d[:, :], in_=s),
           evkey=name, sname="sm_gld", n=16)

    op("scalar", lambda q: q.activation(s_A[:, :], s_A[:, :], AF.Exp),
       evkey="expA", sname="sm_act", waits=("w_alog", "w_cst"))
    op("vector", lambda q: q.tensor_scalar_mul(s_A[:, :], s_A[:, :], -1.0),
       evkey="negA", sname="sm_dve", waits=("expA",))
    op("vector", lambda q: q.tensor_scalar_add(s_D1[:, :], s_D1[:, :], 1.0),
       evkey="D1", sname="sm_dve", waits=("w_dsk", "w_cst"))

    P0T = int(os.environ.get("P0TRUNC", "9"))
    for pc in range(NPC if P0T >= 2 else 0):
        par = pc % 2
        op("gpsimd", lambda q, pc=pc, par=par: q.dma_start(
            out=s_xtt[par].rearrange("p (j t) -> p j t", j=32),
            in_=xT_t.rearrange("(j p) t -> p j t", p=128)[:, :, pc * PCT:(pc + 1) * PCT]),
           evkey=("xtt", pc), sname="sm_x", n=16,
           waits=(() if pc < 2 else (("p0ev", pc - 2),)))
        NJJ = 32 if P0T != 5 else 1
        if P0T != 4:
            for jj in range(NJJ):
                op("tensor", lambda q, jj=jj, par=par: q.matmul(
                    p0_dt, s_wpdt[:, jj * 128:(jj + 1) * 128],
                    s_xtt[par][:, jj * PCT:(jj + 1) * PCT],
                    start=(jj == 0), stop=(jj == NJJ - 1)),
                   evkey=("p0dt", pc) if jj == NJJ - 1 else None, sname="sm_pe",
                   waits=(("w_wpdt", ("xtt", pc)) if jj == 0 else ()) +
                         ((("p0ev", pc - 1),) if (jj == 0 and pc >= 1) else ()))
            for jj in range(NJJ):
                op("tensor", lambda q, jj=jj, par=par: q.matmul(
                    p0_bc, s_wpbc[:, jj * 32:(jj + 1) * 32],
                    s_xtt[par][:, jj * PCT:(jj + 1) * PCT],
                    start=(jj == 0), stop=(jj == NJJ - 1)),
                   evkey=("p0bc", pc) if jj == NJJ - 1 else None, sname="sm_pe",
                   waits=(("w_wpbc",) if jj == 0 else ()))
            op("vector", lambda q: q.tensor_copy(s_agdt, p0_dt),
               waits=(("p0dt", pc),) + ((("agw", pc - 1, 1),) if pc >= 1 else ()))
            op("vector", lambda q: q.tensor_copy(s_agbc, p0_bc),
               evkey=("p0ev", pc), sname="sm_dve", waits=(("p0bc", pc),))
        else:
            op("vector", lambda q: q.memset(s_agdt, 0.125),
               waits=((("xtt", pc),)) + ((("agw", pc - 1, 1),) if pc >= 1 else ()))
            op("vector", lambda q: q.memset(s_agbc, 0.125),
               evkey=("p0ev", pc), sname="sm_dve")
        op("sync", lambda q, pc=pc: q.dma_start(
            out=ag1_in[0:128, pc * PCT:(pc + 1) * PCT], in_=s_agdt),
           evkey=("agw", pc, 0), sname="sm_ld", n=16, waits=(("p0ev", pc),))
        op("sync", lambda q, pc=pc: q.dma_start(
            out=ag1_in[128:160, pc * PCT:(pc + 1) * PCT], in_=s_agbc),
           evkey=("agw", pc, 1), sname="sm_ld", n=16)
    if P0T >= 2:
        EV["ag1w1"] = EV[("agw", NPC - 1, 0)]
        EV["ag1w2"] = EV[("agw", NPC - 1, 1)]
    else:
        op("sync", lambda q: q.dma_start(
            out=ag1_in[0:128, :], in_=xT_t[0:128, :]),
           evkey="ag1w1", sname="sm_ld", n=16, waits=("w_cst",))
        op("sync", lambda q: q.dma_start(
            out=ag1_in[128:160, :], in_=xT_t[0:32, :]),
           evkey="ag1w2", sname="sm_ld", n=16)
    if P0T >= 3:
        op("gpsimd", lambda q: q.collective_compute(
            "AllGather", AL.bypass, replica_groups=[list(range(NRANK))],
            ins=[ag1_in.opt()], outs=[ag1_out.opt()]),
           evkey="ag1", sname="sm_cc", waits=("ag1w1", "ag1w2"))
    else:
        op("gpsimd", lambda q: q.dma_start(out=ag1_out[0:160, :], in_=ag1_in),
           evkey="ag1", sname="sm_gld", n=16, waits=("ag1w1", "ag1w2"))

    EV["p0_sync"] = EV["ag1w2"]
    EV["p0_gp"] = EV["ag1"]
    EV["p0_dve"] = EV.get(("p0ev", NPC - 1), EV["D1"])
    EV["p0_act"] = EV["expA"]
    EV["p0_pe"] = EV.get(("p0dt", NPC - 1), EV["D1"])


    P0W = ("p0_sync", "p0_gp", "p0_dve", "p0_act", "p0_pe")

    def rot_mask(k):
        return [(l - l % 8) + ((l % 8 + k) % 8) for l in range(32)]

    M4, M2, M1 = rot_mask(4), rot_mask(2), rot_mask(1)

    def stage_A(c):
        rnk, hlf = c // 2, c % 2
        csl = slice(hlf * TC, (hlf + 1) * TC)
        w0 = P0W if c == 0 else ()
        op("gpsimd", lambda q, c=c: q.dma_start(
            out=s_x[:, :, :],
            in_=xT_d.rearrange("(j p) t -> p j t", p=128)[:, :, c * TC:(c + 1) * TC]),
           evkey=("xld", c), sname="sm_x", n=16,
           waits=w0 + ((("vstt", c - 1),) if c >= 1 else ()))
        op("gpsimd", lambda q: q.dma_start(
            out=s_dtin[:, :], in_=ag1_out[160 * rnk:160 * rnk + 128, csl]),
           evkey=("dtinld", c), sname="sm_dtin", n=16,
           waits=w0 + ("ag1",) + ((("dtmm", c - 1, NJ - 1),) if c >= 1 else ()))
        op("gpsimd", lambda q: q.dma_start(
            out=s_B[:, :], in_=ag1_out[160 * rnk + 128:160 * rnk + 144, csl]),
           evkey=("Bld", c), sname="sm_bc", n=16,
           waits=((("repB", c - 1, N - 1),) if c >= 1 else ()))
        op("gpsimd", lambda q: q.dma_start(
            out=s_C[:, :], in_=ag1_out[160 * rnk + 144:160 * rnk + 160, csl]),
           evkey=("Cld", c), sname="sm_bc", n=16,
           waits=((("repC", c - 1, N - 1),) if c >= 1 else ()))
        EV[("Bld", c)] = EV[("Cld", c)]

        for j in range(NJ):
            op("tensor", lambda q, j=j: q.matmul(
                p_dtr[:, :], s_dtw[:, j * 128:(j + 1) * 128], s_dtin[:, :],
                start=True, stop=True),
               evkey=("dtmm", c, j), sname="sm_pe",
               waits=w0 + (("dtinld", c), "w_dtw") +
                     ((("sp", c, j - 1),) if j >= 1 else
                      ((("sp", c - 1, NJ - 1), ("repC", c - 1, N - 1),
                        ("repB", c - 1, N - 2)) if c >= 1 else ())))
            op("scalar", lambda q, j=j: q.activation(
                s_dt[:, j, :], p_dtr[:, :], AF.Exp, bias=s_dtb[:, j:j + 1]),
               waits=(("dtmm", c, j),) + ((("uj", c - 1),) if (c >= 1 and j == 0) else ()))
            op("scalar", lambda q, j=j: q.activation(
                s_dt[:, j, :], s_dt[:, j, :], AF.Ln, bias=1.0))
            op("scalar", lambda q, j=j: q.activation(
                s_gsc[:, 11 * 32 + 1:11 * 32 + 2], s_dt[:, j, 0:1], AF.Copy),
               evkey=("sp", c, j), sname="sm_act")
        for n in range(N):
            bank = p_rep if n % 2 == 0 else p_dtr
            op("tensor", lambda q, n=n, bank=bank: q.matmul(
                bank[:, :], s_sel16[:, n * 128:(n + 1) * 128], s_B[:, :],
                start=True, stop=True),
               evkey=("repBmm", c, n), sname="sm_pe",
               waits=(("Bld", c), "w_sel16") +
                     ((("repC", c - 1, N - 1),) if (n == 0 and c >= 1) else ()) +
                     ((("sp", c, NJ - 1),) if n == 1 else ()) +
                     ((("repB", c, n - 2),) if n >= 2 else ()))
            op("scalar", lambda q, n=n, bank=bank: q.activation(
                s_Brep[:, n, :], bank[:, :], AF.Copy),
               evkey=("repB", c, n), sname="sm_act",
               waits=(("repBmm", c, n),) +
                     ((("dbx", c - 1, NJ - 1),) if (n == 0 and c >= 1) else ()))
        op("scalar", lambda q: q.activation(
            s_gsc[:, 11 * 32 + 1:11 * 32 + 2], s_Brep[:, N - 1, 0:1], AF.Copy),
           evkey=("repBbar", c), sname="sm_act")
        EV[("repB", c, N - 1)] = EV[("repBbar", c)]
        for n in range(N):
            bank = p_rep if n % 2 == 0 else p_dtr
            op("tensor", lambda q, n=n, bank=bank: q.matmul(
                bank[:, :], s_sel16[:, n * 128:(n + 1) * 128], s_C[:, :],
                start=True, stop=True),
               evkey=("repCmm", c, n), sname="sm_pe",
               waits=(("Cld", c), ("repB", c, N - 1)) +
                     ((("repC", c, n - 2),) if n >= 2 else ()))
            op("vector", lambda q, n=n, bank=bank: q.tensor_copy(s_Crep[:, n, :], bank[:, :]),
               evkey=("repC", c, n), sname="sm_dve",
               waits=(("repCmm", c, n),) +
                     ((("hc", c - 1, NJ - 1),) if (n == 0 and c >= 1) else ()))

        for j in range(NJ):
            def emit_dA(g, j=j):
                for n in range(g * 8, g * 8 + 8):
                    op("scalar", lambda q, j=j, n=n: q.activation(
                        s_dA[:, n, :], s_dt[:, j, :], AF.Exp,
                        scale=s_A[:, j * N + n:j * N + n + 1]),
                       sname="sm_act",
                       waits=(("negA",) if (j == 0 and g == 0) else ()) +
                             ((("scang", c, j - 1, g),) if j >= 1 else
                              ((("scang", c - 1, NJ - 1, g),) if c >= 1 else ())))
                op("scalar", lambda q, g=g: q.activation(
                    s_gsc[:, 11 * 32 + 1:11 * 32 + 2],
                    s_dA[:, g * 8 + 7, 0:1], AF.Copy),
                   evkey=("dA", c, j, g), sname="sm_act")

            def emit_quarter(g4, j=j, ub=None):
                op("vector", lambda q, ub=ub, g4=g4: q.tensor_mul(
                    s_dbx[:, :, :], ub, s_Brep[:, g4 * 4:(g4 + 1) * 4, :]),
                   evkey=("dbx", c, j) if g4 == 3 else None, sname="sm_dve",
                   waits=((("repB", c, N - 1),) if g4 == 0 else ()))
                for n in range(g4 * 4, g4 * 4 + 4):
                    op("vector", lambda q, j=j, n=n, g4=g4, c=c: q.tensor_tensor_scan(
                        s_h[:, n, :], s_dA[:, n, :], s_dbx[:, n - g4 * 4, :],
                        s_state[1 - c % 2][:, j * N + n:j * N + n + 1],
                        AL.mult, AL.add),
                       evkey=("scang", c, j, n // 8) if n % 8 == 7 else None,
                       sname="sm_dve",
                       waits=(("dA", c, j, n // 8),) if n % 8 == 0 else ())

            emit_dA(0)
            op("vector", lambda q, j=j: q.tensor_mul(
                s_u[:, :], s_dt[:, j, :], s_x[:, j, :]),
               evkey=("uj", c) if j == NJ - 1 else None, sname="sm_dve",
               waits=(("xld", c), ("sp", c, j)) +
                     ((("hc", c, j - 1),) if j >= 1 else ()))
            ub = s_u[:, :].unsqueeze(1).broadcast_to([128, 4, TC])
            emit_quarter(0, ub=ub)
            emit_quarter(1, ub=ub)
            emit_dA(1)
            emit_quarter(2, ub=ub)
            emit_quarter(3, ub=ub)
            op("vector", lambda q, j=j, c=c: q.tensor_copy(
                s_state[c % 2][:, j * N:(j + 1) * N], s_h[:, :, TC - 1:TC]),
               waits=(("scang", c, j, 1),))
            op("vector", lambda q: q.tensor_mul(s_h[:, :, :], s_h[:, :, :], s_Crep[:, :, :]),
               evkey=("hc", c, j), sname="sm_dve", waits=(("repC", c, N - 1),))
            op("vector", lambda q: q.tensor_add(s_h[:, 0:8, :], s_h[:, 0:8, :], s_h[:, 8:16, :]))
            op("vector", lambda q: q.tensor_add(s_h[:, 0:4, :], s_h[:, 0:4, :], s_h[:, 4:8, :]))
            op("vector", lambda q: q.tensor_add(s_h[:, 0:2, :], s_h[:, 0:2, :], s_h[:, 2:4, :]))
            op("vector", lambda q: q.tensor_add(
                s_y[:, :], s_h[:, 0, :], s_h[:, 1, :]))
            op("vector", lambda q, j=j, c=c: q.scalar_tensor_tensor(
                s_v[c % 2][:, j, :], s_x[:, j, :], s_D1[:, j:j + 1], s_y[:, :],
                AL.mult, AL.add),
               evkey=("vstt", c) if j == NJ - 1 else ("vj", c, j), sname="sm_dve",
               waits=("D1",) +
                     ((("summ", c - 2),) if (j == 0 and c >= 2) else ()))

        for j in range(NJ):
            op("tensor", lambda q, j=j, c=c: q.matmul(
                p_stat1[:, :], s_ones16[:, :], s_v[c % 2][:, j, :],
                start=(j == 0), stop=(j == NJ - 1)),
               evkey=("summ", c) if j == NJ - 1 else None, sname="sm_pe",
               waits=((("vj", c, j),) if j < NJ - 1 else (("vstt", c),)) +
                     ((("sqev", c - 1),) if (j == 0 and c >= 1) else ()) +
                     ((("sqev2", c - 3),) if (j == 0 and c >= 3) else ()) +
                     (("w_o16",) if j == 0 else ()))
        op("vector", lambda q: q.tensor_copy(s_st1[:, :], p_stat1[:, :]),
           evkey=("sumev", c), sname="sm_dve",
           waits=(("summ", c),) + ((("ar2w1", c - 3),) if c >= 3 else ()))
        for j in range(NJ):
            sq = s_sqa if j % 2 == 0 else s_sqb
            op("vector", lambda q, j=j, c=c, sq=sq: q.tensor_mul(
                sq[:, :], s_v[c % 2][:, j, :], s_v[c % 2][:, j, :]),
               evkey=("sq", c, j), sname="sm_dve",
               waits=((("sqmm", c, j - 2),) if j >= 2 else ()) +
                     ((("sqmm2", c - 3, j),) if c >= 3 else ()))
            op("tensor", lambda q, j=j, sq=sq: q.matmul(
                p_stat1[:, :], s_ones32[:, :], sq[:, :],
                start=(j == 0), stop=(j == NJ - 1)),
               evkey=("sqmm", c, j), sname="sm_pe",
               waits=(("sq", c, j),) + ((("sumev", c), "w_o32") if j == 0 else ()))
        op("vector", lambda q: q.tensor_copy(s_st2[:, :], p_stat1[:, :]),
           evkey=("sqev", c), sname="sm_dve",
           waits=(("sqmm", c, NJ - 1),) + ((("ar2w2", c - 3),) if c >= 3 else ()))
        op("sync", lambda q, c=c: q.dma_start(out=ar1_i[c][0:1, :], in_=s_st1[:, :]),
           evkey=("ar1w1", c), sname="sm_ld", n=16, waits=(("sumev", c),) + w0)
        op("sync", lambda q, c=c: q.dma_start(out=ar1_i[c][1:2, :], in_=s_st2[:, :]),
           evkey=("ar1w2", c), sname="sm_ld", n=16, waits=(("sqev", c),))
        op("gpsimd", lambda q, c=c: q.collective_compute(
            "AllReduce", AL.add, replica_groups=[list(range(NRANK))],
            ins=[ar1_i[c].opt()], outs=[ar1_o[c].opt()]),
           evkey=("ar1", c), sname="sm_ar1",
           waits=(("ar1w1", c), ("ar1w2", c)) +
                 ((("ar1", c - 1),) if c >= 1 else ()))

    def dram_row_bcast(apx, row):
        r = apx[row:row + 1, :]
        return r.partition_broadcast(128)

    def stage_B(c):
        op("sync", lambda q, c=c: q.dma_start(
            out=s_m1[:, :], in_=dram_row_bcast(ar1_o[c], 0)),
           evkey=("m1ld", c), sname="sm_ld", n=16,
           waits=(("ar1", c),) +
                 ((("applyB", c - 1), ("rstd1", c - 1)) if c >= 1 else ()))
        op("sync", lambda q, c=c: q.dma_start(
            out=s_q1[:, :], in_=dram_row_bcast(ar1_o[c], 1)),
           evkey=("q1ld", c), sname="sm_ld", n=16)
        op("vector", lambda q: q.tensor_scalar_mul(s_m1[:, :], s_m1[:, :], 1.0 / 4096.0),
           waits=(("m1ld", c), ("q1ld", c)))
        op("vector", lambda q: q.tensor_scalar_mul(s_q1[:, :], s_q1[:, :], 1.0 / 4096.0))
        op("vector", lambda q: q.tensor_mul(s_t[:, :], s_m1[:, :], s_m1[:, :]))
        op("vector", lambda q: q.tensor_sub(s_q1[:, :], s_q1[:, :], s_t[:, :]),
           evkey=("varops", c), sname="sm_dve")
        op("scalar", lambda q: q.activation(s_r1[:, :], s_q1[:, :], AF.Ln, bias=1e-5),
           waits=(("varops", c),) + ((("applyB", c - 1),) if c >= 1 else ()))
        op("scalar", lambda q: q.activation(s_r1[:, :], s_r1[:, :], AF.Exp, scale=-0.5),
           evkey=("rstd1", c), sname="sm_act")
        for j in range(NJ):
            op("vector", lambda q, j=j, c=c: q.tensor_sub(
                s_t[:, :], s_v[c % 2][:, j, :], s_m1[:, :]),
               waits=((("rstd1", c),) if j == 0 else ()) +
                     ((("ag2w1", c - 1), ("lpm", c - 1)) if (j == 0 and c >= 1) else ()))
            op("vector", lambda q, j=j, c=c: q.tensor_mul(
                s_ln1b[c % 2][:, j, :], s_t[:, :], s_r1[:, :]),
               evkey=("applyB", c) if j == NJ - 1 else ("apl", c, j), sname="sm_dve")
        for j in range(NJ):
            op("tensor", lambda q, j=j, c=c: q.matmul(
                p_gl[0:8, 0:TC], s_wg[:, j * 8:(j + 1) * 8], s_ln1b[c % 2][:, j, :],
                start=(j == 0), stop=(j == NJ - 1)),
               evkey=("gpm", c) if j == NJ - 1 else None, sname="sm_pe",
               waits=((("apl", c, j),) if j < NJ - 1 else (("applyB", c),)) +
                     (("w_wg",) if j == 0 else ()) +
                     ((("glev", c - 1),) if (j == 0 and c >= 1) else ()))
        for j in range(NJ):
            op("tensor", lambda q, j=j, c=c: q.matmul(
                p_gl[:, TC:2 * TC], s_wl[:, j * 128:(j + 1) * 128], s_ln1b[c % 2][:, j, :],
                start=(j == 0), stop=(j == NJ - 1)),
               evkey=("lpm", c) if j == NJ - 1 else None, sname="sm_pe",
               waits=(("w_wl",) if j == 0 else ()))
        op("vector", lambda q: q.tensor_copy(s_gp[:, :], p_gl[0:8, 0:TC]),
           evkey=("gpev", c), sname="sm_dve", waits=(("gpm", c),))
        op("vector", lambda q: q.tensor_copy(s_lowp[:, :], p_gl[:, TC:2 * TC]),
           evkey=("glev", c), sname="sm_dve", waits=(("lpm", c),))
        op("sync", lambda q, c=c: q.dma_start(
            out=ag2_in[c][0:512, :].rearrange("(j p) t -> p j t", p=128),
            in_=s_ln1b[c % 2][:, :, :]),
           evkey=("ag2w1", c), sname="sm_ld", n=16, waits=(("applyB", c),))
        op("sync", lambda q, c=c: q.dma_start(
            out=ag2_in[c][512:640, :], in_=s_lowp[:, :]),
           evkey=("ag2w2", c), sname="sm_ld", n=16, waits=(("glev", c),))
        op("sync", lambda q, c=c: q.dma_start(
            out=ag2_in[c][640:656, :].rearrange("(a b) t -> a (b t)", b=2).bitcast(F32),
            in_=s_gp[:, :]),
           evkey=("ag2w3", c), sname="sm_ld", n=16, waits=(("gpev", c),))
        op("gpsimd", lambda q, c=c: q.collective_compute(
            "AllGather", AL.bypass, replica_groups=[list(range(NRANK))],
            ins=[ag2_in[c].opt()], outs=[ag2_out[c].opt()]),
           evkey=("ag2", c), sname="sm_ag2",
           waits=(("ag2w1", c), ("ag2w2", c), ("ag2w3", c)) +
                 ((("ag2", c - 1),) if c >= 1 else ()))

    def stage_C(c):
        for r in range(NRANK):
            op("sync", lambda q, r=r, c=c: q.dma_start(
                out=s_gbf[8 * r:8 * r + 8, :],
                in_=ag2_out[c][656 * r + 640:656 * r + 656, :]
                    .rearrange("(a b) t -> a (b t)", b=2).bitcast(F32)),
               evkey=("gbld", c) if r == NRANK - 1 else ("gbldp", c, r), sname="sm_ld", n=16,
               waits=((("ag2", c),) if r == 0 else ()) +
                     ((("gsmm", c - 1),) if (r == 0 and c >= 1) else ()))
        op("tensor", lambda q: q.matmul(
            p_wrep[0:8, :], s_selG[:, :], s_gbf[:, :], start=True, stop=True),
           evkey=("gsmm", c), sname="sm_pe",
           waits=(("gbld", c), "w_selG") +
                 ((("wrepev", c - 1),) if c >= 1 else ()))
        op("vector", lambda q: q.tensor_copy(s_gsum[:, :], p_wrep[0:8, :]),
           evkey=("gsev", c), sname="sm_dve",
           waits=(("gsmm", c),) + ((("gdw", c - 1),) if c >= 1 else ()))
        op("sync", lambda q: q.dma_start(out=gdram, in_=s_gsum[:, :]),
           evkey=("gdw", c), sname="sm_ld", n=16, waits=(("gsev", c),))
        for e in range(8):
            op("sync", lambda q, e=e: q.dma_start(
                out=s_gpk[e::8, :],
                in_=gdram[e:e + 1, :].rearrange("o (q t) -> (o q) t", t=32)),
               evkey=("gpkld", c) if e == 7 else ("gpkldp", c, e),
               sname="sm_ld", n=16,
               waits=(("gdw", c),) + ((("gateops", c - 1),) if (e == 0 and c >= 1) else ()))
        G = s_gpk
        sc = lambda i: s_gsc[:, i * 32:(i + 1) * 32]
        gseq = [0]

        def gop(fn, waits=(), ev=None, eng="vector"):
            key = ("g", c, gseq[0])
            gseq[0] += 1
            prev = (("g", c, gseq[0] - 2),) if gseq[0] >= 2 else ()
            op(eng, fn, evkey=(ev or key), sname="sm_dve" if eng == "vector" else "sm_act",
               waits=prev + waits)

        gop(lambda q: q.stream_shuffle(sc(0)[:, :], G[:, :], M4), waits=(("gpkld", c),))
        gop(lambda q: q.tensor_max(sc(1)[:, :], G[:, :], sc(0)[:, :]))
        gop(lambda q: q.stream_shuffle(sc(0)[:, :], sc(1)[:, :], M2))
        gop(lambda q: q.tensor_max(sc(1)[:, :], sc(1)[:, :], sc(0)[:, :]))
        gop(lambda q: q.stream_shuffle(sc(0)[:, :], sc(1)[:, :], M1))
        gop(lambda q: q.tensor_max(sc(1)[:, :], sc(1)[:, :], sc(0)[:, :]))
        gop(lambda q: q.tensor_tensor(sc(2)[:, :], G[:, :], sc(1)[:, :], AL.is_equal))
        gop(lambda q: q.scalar_tensor_tensor(
            sc(3)[:, :], sc(2)[:, :], -1e30, G[:, :], AL.mult, AL.add))
        gop(lambda q: q.stream_shuffle(sc(0)[:, :], sc(3)[:, :], M4))
        gop(lambda q: q.tensor_max(sc(3)[:, :], sc(3)[:, :], sc(0)[:, :]))
        gop(lambda q: q.stream_shuffle(sc(0)[:, :], sc(3)[:, :], M2))
        gop(lambda q: q.tensor_max(sc(3)[:, :], sc(3)[:, :], sc(0)[:, :]))
        gop(lambda q: q.stream_shuffle(sc(0)[:, :], sc(3)[:, :], M1))
        gop(lambda q: q.tensor_max(sc(3)[:, :], sc(3)[:, :], sc(0)[:, :]))
        gop(lambda q: q.tensor_sub(sc(4)[:, :], sc(3)[:, :], sc(1)[:, :]))
        gop(lambda q: q.activation(sc(5)[:, :], sc(4)[:, :], AF.Exp), eng="scalar")
        gop(lambda q: q.tensor_scalar(
            sc(5)[:, :], sc(5)[:, :], 0.5, 0.5, AL.mult, AL.add))
        gop(lambda q: q.reciprocal(sc(6)[:, :], sc(5)[:, :]))
        gop(lambda q: q.tensor_sub(sc(7)[:, :], G[:, :], sc(1)[:, :]))
        gop(lambda q: q.activation(sc(8)[:, :], sc(7)[:, :], AF.Exp), eng="scalar")
        gop(lambda q: q.tensor_tensor(sc(9)[:, :], G[:, :], sc(3)[:, :], AL.is_ge))
        gop(lambda q: q.tensor_mul(sc(10)[:, :], sc(8)[:, :], sc(9)[:, :]))
        gop(lambda q: q.tensor_mul(s_wpk[:, :], sc(10)[:, :], sc(6)[:, :]),
            waits=((("wdw", c - 1),) if c >= 1 else ()), ev=("gateops", c))
        for e in range(8):
            op("sync", lambda q, e=e: q.dma_start(
                out=wdram[e:e + 1, :].rearrange("o (q t) -> (o q) t", t=32),
                in_=s_wpk[e::8, :]),
               evkey=("wdw", c) if e == 7 else ("wdwp", c, e),
               sname="sm_ld", n=16, waits=(("gateops", c),) if e == 0 else ())
        op("sync", lambda q: q.dma_start(out=s_w8[:, :], in_=wdram),
           evkey=("w8ld", c), sname="sm_ld", n=16,
           waits=(("wdw", c),) + ((("wrepmm", c - 1),) if c >= 1 else ()))
        op("tensor", lambda q: q.matmul(
            p_wrep[:, :], s_selW[:, :], s_w8[:, :], start=True, stop=True),
           evkey=("wrepmm", c), sname="sm_pe",
           waits=(("w8ld", c), "w_selW", ("gsev", c)))
        op("vector", lambda q: q.tensor_copy(s_wrep[:, :], p_wrep[:, :]),
           evkey=("wrepev", c), sname="sm_dve", waits=(("wrepmm", c),))
        for w in range(4):
            for k in range(2):
                r = 2 * w + k
                op("sync", lambda q, r=r, k=k, c=c: q.dma_start(
                    out=s_lowblk[:, k, :],
                    in_=ag2_out[c][656 * r + 512:656 * r + 640, :]),
                   evkey=("lwld", c, w) if k == 1 else ("lwldp", c, w, k), sname="sm_ld", n=16,
                   waits=((("ag2", c),) if (w == 0 and k == 0) else ()) +
                         ((("lws", c, w - 1),) if (w >= 1 and k == 0) else ()) +
                         ((("lws", c - 1, 3),) if (w == 0 and k == 0 and c >= 1) else ()))
            if w == 0:
                op("vector", lambda q: q.tensor_add(
                    s_lowacc[:, :], s_lowblk[:, 0, :], s_lowblk[:, 1, :]),
                   evkey=("lws", c, 0), sname="sm_dve", waits=(("lwld", c, 0),))
            else:
                op("vector", lambda q, w=w, c=c: q.tensor_add(
                    s_lowacc[:, :], s_lowacc[:, :], s_lowblk[:, 0, :]),
                   waits=(("lwld", c, w),))
                op("vector", lambda q, w=w, c=c: q.tensor_add(
                    s_lowacc[:, :], s_lowacc[:, :], s_lowblk[:, 1, :]),
                   evkey=("lws", c, w), sname="sm_dve")
        op("vector", lambda q: q.tensor_mul(s_lowW[:, :], s_lowacc[:, :], s_wrep[:, :]),
           evkey=("lowW", c), sname="sm_dve",
           waits=(("lws", c, 3), ("wrepev", c)) +
                 ((("ffnmm", c - 1, 1, NJ - 1),) if c >= 1 else ()))

        for h in range(2):
            tsl2 = slice(h * HF, (h + 1) * HF)
            for r in range(NRANK):
                op("sync", lambda q, r=r, c=c, h=h: q.dma_start(
                    out=s_rhs[:, 4 * r:4 * r + 4, :],
                    in_=ag2_out[c][656 * r:656 * r + 512, h * HF:(h + 1) * HF]
                        .rearrange("(j p) t -> p j t", p=128)),
                   evkey=("rhsld", c, h) if r == NRANK - 1 else ("rhsldp", c, h, r),
                   sname="sm_ld", n=16,
                   waits=((("ag2", c),) if r == 0 else ()) +
                         ((("ffnmm", c, h - 1, 3),) if (r == 0 and h == 1) else ()) +
                         ((("ffnmm", c - 1, 1, 3),) if (r == 0 and h == 0 and c >= 1) else ()))
            for i in range(NJ):
                reg = slice((i % 2) * TC, (i % 2) * TC + HF)
                for jj in range(32):
                    op("tensor", lambda q, i=i, jj=jj, reg=reg: q.matmul(
                        p_ffn[:, reg], s_basew[:, jj * 512 + i * 128:jj * 512 + (i + 1) * 128],
                        s_rhs[:, jj, :], start=(jj == 0), stop=False),
                       sname="sm_pe",
                       waits=((("rhsld", c, h), "w_basew") if (i == 0 and jj == 0) else ()) +
                             ((("ffnev", c, h, i - 2),) if (jj == 0 and i >= 2) else ()) +
                             ((("ffnev", c, 0, i + 2),) if (jj == 0 and i < 2 and h == 1) else ()) +
                             ((("ffnev", c - 1, 1, i + 2),) if (jj == 0 and i < 2 and h == 0 and c >= 1) else ()))
                op("tensor", lambda q, i=i, reg=reg, tsl2=tsl2: q.matmul(
                    p_ffn[:, reg], s_lorb[:, i * 128:(i + 1) * 128],
                    s_lowW[:, tsl2], start=False, stop=True),
                   evkey=("ffnmm", c, h, i), sname="sm_pe",
                   waits=(("lowW", c), "w_lorb"))
                op("vector", lambda q, i=i, reg=reg, tsl2=tsl2: q.tensor_scalar(
                    s_ffn[:, i, tsl2], p_ffn[:, reg], s_bb[:, i:i + 1], None, AL.add),
                   evkey=("ffnev", c, h, i), sname="sm_dve",
                   waits=(("ffnmm", c, h, i), "w_bb"))
            op("vector", lambda q, c=c, tsl2=tsl2: q.tensor_add(
                s_w2[c % 2][:, :, tsl2], s_ffn[:, :, tsl2], s_ln1b[c % 2][:, :, tsl2]),
               evkey=("resid", c, h), sname="sm_dve",
               waits=(("ffnev", c, h, NJ - 1),) +
                     ((("sqmm2", c - 2, NJ - 1),) if (h == 0 and c >= 2) else ()))
        for j in range(NJ):
            op("tensor", lambda q, j=j, c=c: q.matmul(
                p_stat1[:, :], s_ones16[:, :], s_w2[c % 2][:, j, :],
                start=(j == 0), stop=(j == NJ - 1)),
               evkey=("summ2", c) if j == NJ - 1 else None, sname="sm_pe",
               waits=((("resid", c, 1),) if j == 0 else ()) +
                     ((("sqev", c + 2),) if (j == 0 and c + 2 < NCH) else ()) +
                     ((("sqev2", c - 1),) if (j == 0 and c >= 1) else ()))
        op("vector", lambda q: q.tensor_copy(s_st1[:, :], p_stat1[:, :]),
           evkey=("sumev2", c), sname="sm_dve",
           waits=(("summ2", c),) + ((("ar1w1", c + 2),) if c + 2 < NCH else ()))
        for j in range(NJ):
            sq = s_sqa if j % 2 == 0 else s_sqb
            op("vector", lambda q, j=j, c=c, sq=sq: q.tensor_mul(
                sq[:, :], s_w2[c % 2][:, j, :], s_w2[c % 2][:, j, :]),
               evkey=("sq2", c, j), sname="sm_dve",
               waits=((("sqmm2", c, j - 2),) if j >= 2 else ()) +
                     ((("sqmm", c + 2, j),) if (j == 0 and c + 2 < NCH) else ()))
            op("tensor", lambda q, j=j, sq=sq: q.matmul(
                p_stat1[:, :], s_ones32[:, :], sq[:, :],
                start=(j == 0), stop=(j == NJ - 1)),
               evkey=("sqmm2", c, j), sname="sm_pe",
               waits=(("sq2", c, j),) + ((("sumev2", c),) if j == 0 else ()))
        op("vector", lambda q: q.tensor_copy(s_st2[:, :], p_stat1[:, :]),
           evkey=("sqev2", c), sname="sm_dve",
           waits=(("sqmm2", c, NJ - 1),) + ((("ar1w2", c + 2),) if c + 2 < NCH else ()))
        op("sync", lambda q, c=c: q.dma_start(out=ar2_i[c][0:1, :], in_=s_st1[:, :]),
           evkey=("ar2w1", c), sname="sm_ld", n=16, waits=(("sumev2", c),))
        op("sync", lambda q, c=c: q.dma_start(out=ar2_i[c][1:2, :], in_=s_st2[:, :]),
           evkey=("ar2w2", c), sname="sm_ld", n=16, waits=(("sqev2", c),))
        op("gpsimd", lambda q, c=c: q.collective_compute(
            "AllReduce", AL.add, replica_groups=[list(range(NRANK))],
            ins=[ar2_i[c].opt()], outs=[ar2_o[c].opt()]),
           evkey=("ar2", c), sname="sm_ar2",
           waits=(("ar2w1", c), ("ar2w2", c)) +
                 ((("ar2", c - 1),) if c >= 1 else ()))

    def stage_D(c):
        op("sync", lambda q, c=c: q.dma_start(
            out=s_m2[:, :], in_=dram_row_bcast(ar2_o[c], 0)),
           evkey=("m2ld", c), sname="sm_ld", n=16,
           waits=(("ar2", c),) +
                 ((("applyD", c - 1, NJ - 1), ("rstd2", c - 1)) if c >= 1 else ()))
        op("sync", lambda q, c=c: q.dma_start(
            out=s_q2[:, :], in_=dram_row_bcast(ar2_o[c], 1)),
           evkey=("q2ld", c), sname="sm_ld", n=16)
        op("vector", lambda q: q.tensor_scalar_mul(s_m2[:, :], s_m2[:, :], 1.0 / 4096.0),
           waits=(("m2ld", c), ("q2ld", c)))
        op("vector", lambda q: q.tensor_scalar_mul(s_q2[:, :], s_q2[:, :], 1.0 / 4096.0))
        op("vector", lambda q: q.tensor_mul(s_t[:, :], s_m2[:, :], s_m2[:, :]))
        op("vector", lambda q: q.tensor_sub(s_q2[:, :], s_q2[:, :], s_t[:, :]),
           evkey=("varops2", c), sname="sm_dve")
        op("scalar", lambda q: q.activation(s_r2[:, :], s_q2[:, :], AF.Ln, bias=1e-5),
           waits=(("varops2", c),) + ((("rstd1", c + 2),) if c + 2 < NCH else ()))
        op("scalar", lambda q: q.activation(s_r2[:, :], s_r2[:, :], AF.Exp, scale=-0.5),
           evkey=("rstd2", c), sname="sm_act")
        for j in range(NJ):
            op("vector", lambda q, j=j, c=c: q.tensor_sub(
                s_t[:, :], s_w2[c % 2][:, j, :], s_m2[:, :]),
               waits=((("rstd2", c),) if j == 0 else ()) +
                     ((("applyB", c + 2),) if (j == 0 and c + 2 < NCH) else ()) +
                     ((("outw", c - 1, j),) if c >= 1 else ()))
            op("vector", lambda q, j=j: q.tensor_mul(s_out[:, :], s_t[:, :], s_r2[:, :]),
               evkey=("applyD", c, j), sname="sm_dve",
               waits=(((("outw", c, j - 1),) if j >= 1 else
                       ((("outw", c - 1, NJ - 1),) if c >= 1 else ()))))
            op("sync", lambda q, c=c, j=j: q.dma_start(
                out=out_t.rearrange("(j p) t -> p j t", p=128)[:, j, c * TC:(c + 1) * TC],
                in_=s_out[:, :]),
               evkey=("outw", c, j), sname="sm_out", n=16, waits=(("applyD", c, j),))
        EV[("applyD", c)] = EV[("applyD", c, NJ - 1)]

    op("vector", lambda q: q.memset(s_state[0][:, :], 0.0), waits=P0W)
    op("vector", lambda q: q.memset(s_state[1][:, :], 0.0),
       evkey="st0", sname="sm_dve")
    EV[("scang", -1, NJ - 1, 1)] = EV["st0"]

    LIM = int(os.environ.get("KTRUNC", str(NCH + 3)))
    for i in range(min(NCH + 3, LIM)):
        if i < NCH:
            stage_A(i)
        if 0 <= i - 1 < NCH:
            stage_B(i - 1)
        if 0 <= i - 2 < NCH:
            stage_C(i - 2)
        if 0 <= i - 3 < NCH:
            stage_D(i - 3)

    if KDUMP == "ag1":
        op("gpsimd", lambda q: q.dma_start(out=dbg, in_=ag1_out),
           evkey="dump", sname="sm_gld", n=16, waits=("ag1",))
    elif KDUMP in ("dt", "v", "ln1b", "h", "u", "brep", "w8", "ffn", "w2", "dA", "gsum", "gpk", "wpk", "gbf", "gsc", "r2", "m2", "outd", "state0"):
        srcmap = {
            "dt": (s_dt[:, :, :].rearrange("p j t -> p (j t)"), 128, 2048, "vstt"),
            "v": (s_v[0][:, :, :].rearrange("p j t -> p (j t)"), 128, 2048, "vstt"),
            "h": (s_h[:, :, :].rearrange("p n t -> p (n t)"), 128, 8192, "vstt"),
            "dA": (s_dA[:, :, :].rearrange("p n t -> p (n t)"), 128, 8192, "vstt"),
            "u": (s_u[:, :], 128, 512, "vstt"),
            "brep": (s_Brep[:, :, :].rearrange("p n t -> p (n t)"), 128, 8192, "vstt"),
            "ln1b": (s_ln1b[0][:, :, :].rearrange("p j t -> p (j t)"), 128, 2048, ("applyB", 0)),
            "w8": (s_w8[:, :], 8, 512, ("w8ld", 0)),
            "gsum": (s_gsum[:, :], 8, 512, ("gsev", 0)),
            "gpk": (s_gpk[:, :], 128, 32, ("gpkld", 0)),
            "wpk": (s_wpk[:, :], 128, 32, ("gateops", 0)),
            "gbf": (s_gbf[:, :], 64, 512, ("gbld", 0)),
            "gsc": (s_gsc[:, :], 128, 384, ("gateops", 0)),
            "r2": (s_r2[:, :], 128, 512, ("rstd2", 0)),
            "m2": (s_m2[:, :], 128, 512, ("rstd2", 0)),
            "outd": (s_out[:, :], 128, 512, ("applyD", 0, 3)),
            "state0": (s_state[0][:, :], 128, 64, ("vstt", 0)),
            "ffn": (s_ffn[:, :, :].rearrange("p j t -> p (j t)"), 128, 2048, ("resid", 0, 1)),
            "w2": (s_w2[0][:, :, :].rearrange("p j t -> p (j t)"), 128, 2048, ("resid", 0, 1)),
        }
        sap, pp, ff, wk = srcmap[KDUMP]
        if wk == "vstt":
            wk = ("vstt", 0)
        nrow = (pp * ff * 4 // 4) // TSL
        op("gpsimd", lambda q, sap=sap, pp=pp, ff=ff: q.dma_start(
            out=dbg[0:pp, 0:ff] if ff <= TSL else
                dbg[0:pp * (ff // TSL), :].rearrange("(p k) t -> p (k t)", k=ff // TSL),
            in_=sap), evkey="dump", sname="sm_gld", n=16, waits=(wk,))

    def _final(q):
        q.wait_ge(SM["sm_out"], CNT["sm_out"])
    sched["sync"].append(_final)

    with nc.Block() as blk:
        @blk.sync
        def _(q):
            for fn in sched["sync"]:
                fn(q)

        @blk.gpsimd
        def _(q):
            for fn in sched["gpsimd"]:
                fn(q)

        @blk.vector
        def _(q):
            for fn in sched["vector"]:
                fn(q)

        @blk.scalar
        def _(q):
            for fn in sched["scalar"]:
                fn(q)

        @blk.tensor
        def _(q):
            for fn in sched["tensor"]:
                fn(q)

    es.close()
    return nc


_NC_CACHE = None


def kernel(L0_attn_x, L0_attn_A_log, L0_attn_D, L0_attn_x_proj_w,
           L0_attn_dt_proj_w, L0_attn_dt_proj_b, L0_ffn_gate_w,
           L0_ffn_base_w, L0_ffn_base_b, L0_ffn_lora_A, L0_ffn_lora_B):
    f32 = np.float32
    x = np.asarray(L0_attn_x, f32)[0]
    xT = np.ascontiguousarray(x.T)
    xpw = np.asarray(L0_attn_x_proj_w, f32)
    dtpw = np.asarray(L0_attn_dt_proj_w, f32)
    dtpb = np.asarray(L0_attn_dt_proj_b, f32)
    alogf = np.asarray(L0_attn_A_log, f32)
    dskf = np.asarray(L0_attn_D, f32)
    gwf = np.asarray(L0_ffn_gate_w, f32)
    bwf = np.asarray(L0_ffn_base_w, f32)
    bbf = np.asarray(L0_ffn_base_b, f32)
    lAf = np.asarray(L0_ffn_lora_A, f32)
    lBf = np.asarray(L0_ffn_lora_B, f32)

    wpdt = np.ascontiguousarray(
        xpw[0:128].reshape(128, 32, 128).transpose(2, 1, 0).reshape(128, 32 * 128))
    wpbc = np.ascontiguousarray(
        xpw[128:160].reshape(32, 32, 128).transpose(2, 1, 0).reshape(128, 32 * 32))
    import ml_dtypes
    sel16 = np.zeros((16, N * 128), ml_dtypes.bfloat16)
    for n in range(N):
        sel16[n, n * 128:(n + 1) * 128] = 1.0
    selWh = np.zeros((8, 128), f32)
    for p in range(128):
        selWh[p // 16, p] = 1.0
    selGh = np.zeros((64, 8), f32)
    for r in range(8):
        for e in range(8):
            selGh[8 * r + e, e] = 1.0
    ones16 = np.ones((128, 1), np.float16)
    ones32 = np.ones((128, 1), f32)
    csth = np.tile(np.asarray([CONSTS], f32), (128, 1))
    lA2 = lAf.reshape(128, D)

    in_maps = []
    for k in range(NRANK):
        dsl = slice(DL * k, DL * (k + 1))
        osl = dsl
        tsl = slice(TSL * k, TSL * (k + 1))
        bw_k = np.ascontiguousarray(
            bwf[osl, :].T.reshape(32, 128, 4, 128).transpose(1, 0, 2, 3)
            .reshape(128, 32 * 512))
        lb_k = np.ascontiguousarray(
            lBf[:, osl, :].transpose(0, 2, 1).reshape(128, OL))
        m = dict(
            xT_d=np.ascontiguousarray(xT[dsl, :]),
            xT_t=np.ascontiguousarray(xT[:, tsl]),
            wpdt=wpdt, wpbc=wpbc,
            dtw=np.ascontiguousarray(dtpw[dsl, :].T),
            dtb=np.ascontiguousarray(dtpb[dsl].reshape(NJ, 128).T),
            alog=np.ascontiguousarray(
                alogf[dsl].reshape(NJ, 128, N).transpose(1, 0, 2).reshape(128, NJ * N)),
            dsk=np.ascontiguousarray(dskf[dsl].reshape(NJ, 128).T),
            wg=np.ascontiguousarray(
                gwf[:, dsl].T.reshape(NJ, 128, 8).transpose(1, 0, 2).reshape(128, NJ * 8)),
            wl=np.ascontiguousarray(
                lA2[:, dsl].T.reshape(NJ, 128, 128).transpose(1, 0, 2).reshape(128, NJ * 128)),
            bw=bw_k, lb=lb_k,
            bb=np.ascontiguousarray(bbf[osl].reshape(NJ, 128).T),
            sel16=sel16, selW=selWh, selG=selGh,
            ones16=ones16, ones32=ones32, cst=csth,
        )
        in_maps.append(m)

    nc = build()
    res = run_bass_kernel_spmd(nc, in_maps, core_ids=list(range(NRANK)))
    kernel._res = res
    parts = [np.asarray(res.results[k]["out_t"], f32) for k in range(NRANK)]
    full = np.concatenate(parts, axis=0)
    return np.ascontiguousarray(full.T)[None]
